# version 02
# baseline (speedup 1.0000x reference)
import numpy as np
from contextlib import ExitStack

import concourse.bass as bass
import concourse.mybir as mybir
from concourse.bass_utils import run_bass_kernel_spmd

F32 = mybir.dt.float32
F32R = mybir.dt.float32r
BF16 = mybir.dt.bfloat16
AF = mybir.ActivationFunctionType
ALU = mybir.AluOpType
AX = mybir.AxisListType

D = 1024
DI = 2048
NST = 16
RK = 64
KD = D // 128
KI = DI // 128
TC = 512
NZ = 8
DFF = 4096
KF = DFF // 128
RH = 256
EPS = 1e-5


class Sem:
    def __init__(self, es, nc, name):
        self.h = es.enter_context(nc.semaphore(name))
        self.n = 0


def build(L, debug=False, serialize=False):
    import os as _osd
    dbg_heavy = debug and not _osd.environ.get("NOHEAVY")
    nc = bass.Bass("TRN2", target_bir_lowering=False)
    NT = L // TC

    xT = nc.dram_tensor("xT", [D, L], F32, kind="ExternalInput").ap()
    w_inT = nc.dram_tensor("w_inT", [16, KD, 128, 2, 128], F32, kind="ExternalInput").ap()
    w_outT = nc.dram_tensor("w_outT", [4, KI, 128, 2, 128], F32, kind="ExternalInput").ap()
    w_xpT = nc.dram_tensor("w_xpT", [128, KI, 128], F32, kind="ExternalInput").ap()
    w_dtT = nc.dram_tensor("w_dtT", [RK, DI], F32, kind="ExternalInput").ap()
    convw = nc.dram_tensor("convw", [128, KI, 4], F32, kind="ExternalInput").ap()
    convb = nc.dram_tensor("convb", [128, KI], F32, kind="ExternalInput").ap()
    dtb = nc.dram_tensor("dtb", [128, KI], F32, kind="ExternalInput").ap()
    alog = nc.dram_tensor("alog", [128, KI, NST], F32, kind="ExternalInput").ap()
    dsk = nc.dram_tensor("dsk", [128, KI], F32, kind="ExternalInput").ap()
    w1h = nc.dram_tensor("w1h", [D, RH], F32, kind="ExternalInput").ap()
    b1c = nc.dram_tensor("b1c", [128, 2], F32, kind="ExternalInput").ap()
    w2h = nc.dram_tensor("w2h", [RH, 8], F32, kind="ExternalInput").ap()
    b2r = nc.dram_tensor("b2r", [1, 8], F32, kind="ExternalInput").ap()
    wah = nc.dram_tensor("wah", [NZ, 16, KD, 128, 2, 128], F32, kind="ExternalInput").ap()
    wbh = nc.dram_tensor("wbh", [NZ, 4, KF, 128, 2, 128], F32, kind="ExternalInput").ap()
    cst = nc.dram_tensor("cst", [128, 8], F32, kind="ExternalInput").ap()
    cstMS = nc.dram_tensor("cstMS", [128, 256], F32, kind="ExternalInput").ap()
    z2x128 = nc.dram_tensor("z2x128", [128, 256], F32, kind="ExternalInput").ap()
    eyeh = nc.dram_tensor("eyeh", [16, NST], F32, kind="ExternalInput").ap()
    ninf = nc.dram_tensor("ninf", [1, 8], F32, kind="ExternalInput").ap()
    onesrow_h = nc.dram_tensor("onesrow", [1, 128], F32, kind="ExternalInput").ap()

    w_inB = nc.dram_tensor("w_inB", [16, KD, 128, 2, 128], BF16).ap()
    w_outB = nc.dram_tensor("w_outB", [4, KI, 128, 2, 128], BF16).ap()
    waB = nc.dram_tensor("waB", [NZ, 16, KD, 128, 2, 128], BF16).ap()
    wbB = nc.dram_tensor("wbB", [NZ, 4, KF, 128, 2, 128], BF16).ap()
    if debug:
        x2s = nc.dram_tensor("x2s", [128, KD, L], F32, kind="ExternalOutput").ap()
        xn3s = nc.dram_tensor("xn3s", [128, KD, L], F32, kind="ExternalOutput").ap()
    else:
        x2s = nc.dram_tensor("x2s", [128, KD, L], F32).ap()
        xn3s = nc.dram_tensor("xn3s", [128, KD, L], F32).ap()
    out = nc.dram_tensor("out", [128, KD, L], F32, kind="ExternalOutput").ap()
    if debug:
        dbg_u = nc.dram_tensor("dbg_u", [128, KI, L], F32, kind="ExternalOutput").ap()
        dbg_dl = nc.dram_tensor("dbg_dl", [128, KI, L], F32, kind="ExternalOutput").ap()
        dbg_y = nc.dram_tensor("dbg_y", [128, KI, L], F32, kind="ExternalOutput").ap()
        dbg_w = nc.dram_tensor("dbg_w", [128, 8], F32, kind="ExternalOutput").ap()
        dbg_x2 = nc.dram_tensor("dbg_x2", [128, KD, L], F32, kind="ExternalOutput").ap()
        dbg_xn1 = nc.dram_tensor("dbg_xn1", [128, KD, L], BF16, kind="ExternalOutput").ap()
        dbg_rr = nc.dram_tensor("dbg_rr", [1, 64], F32, kind="ExternalOutput").ap()
        dbg_pool = nc.dram_tensor("dbg_pool", [128, KD], F32, kind="ExternalOutput").ap()
        dbg_h = nc.dram_tensor("dbg_h", [128, 2], F32, kind="ExternalOutput").ap()
        dbg_xh = nc.dram_tensor("dbg_xh", [128, KI, L], F32, kind="ExternalOutput").ap()

    es = ExitStack()
    e = es.enter_context
    s1 = ExitStack()
    e1 = s1.enter_context

    bx = e(nc.sbuf_tensor("bx", [128, KD, TC], F32R))
    bp6 = e(nc.sbuf_tensor("bp6", [128, 3, TC], F32))
    brow = e(nc.sbuf_tensor("brow", [1, 3, TC], F32))
    browR = e(nc.sbuf_tensor("browR", [1, 2, TC], F32R))
    bsq = e(nc.sbuf_tensor("bsq", [128, 2, TC], F32R))
    bred = e(nc.sbuf_tensor("bred", [128, KD], F32))
    bpool = e(nc.sbuf_tensor("bpool", [128, KD], F32))
    bcstF = e(nc.sbuf_tensor("bcstF", [128, 8], F32))
    bcstR = e(nc.sbuf_tensor("bcstR", [128, 8], F32R))
    bcstMS = e(nc.sbuf_tensor("bcstMS", [128, 256], F32R))
    bonesrow = e(nc.sbuf_tensor("bonesrow", [1, 128], F32R))
    bninf = e(nc.sbuf_tensor("bninf", [1, 8], F32))
    bb2 = e(nc.sbuf_tensor("bb2", [1, 8], F32))
    bb1 = e(nc.sbuf_tensor("bb1", [128, 2], F32))
    bwz = e(nc.sbuf_tensor("bwz", [128, 8], F32))
    bpoolr = e(nc.sbuf_tensor("bpoolr", [128, KD], F32R))

    bxn1 = e1(nc.sbuf_tensor("bxn1", [128, KD, TC], BF16))
    bu = e1(nc.sbuf_tensor("bu", [128, KI, TC], F32))
    bdelta = e1(nc.sbuf_tensor("bdelta", [128, KI, TC], F32))
    by = e1(nc.sbuf_tensor("by", [128, KI, TC], F32))
    byb = e1(nc.sbuf_tensor("byb", [128, KI, TC], BF16))
    bmo = bdelta[:, 0:KD, :]
    bconv = e1(nc.sbuf_tensor("bconv", [128, 1, TC + 4], F32))
    bhalo = e1(nc.sbuf_tensor("bhalo", [128, KI, 3], F32))
    bda = e1(nc.sbuf_tensor("bda", [128, 1, TC], F32))
    bdbu = e1(nc.sbuf_tensor("bdbu", [128, 2, TC], F32))
    bhh = e1(nc.sbuf_tensor("bhh", [128, 2, TC], F32))
    bcrep = e1(nc.sbuf_tensor("bcrep", [128, TC], F32))
    bcar = e1(nc.sbuf_tensor("bcar", [128, KI, NST], F32))
    bdtr = e1(nc.sbuf_tensor("bdtr", [64, TC], BF16))
    bBC = e1(nc.sbuf_tensor("bBC", [16, TC], F32R))
    bCsb = e1(nc.sbuf_tensor("bCsb", [16, TC], F32R))
    beye = e1(nc.sbuf_tensor("beye", [16, NST], F32R))
    bwr = e1(nc.sbuf_tensor("bwr", [128, 2, KI, 2, 128], BF16))
    bxpj = e1(nc.sbuf_tensor("bxpj", [128, KI, 128], F32R))
    bdtp = e1(nc.sbuf_tensor("bdtp", [64, DI], BF16))
    bA = e1(nc.sbuf_tensor("bA", [128, KI, NST], F32))
    bdtb2 = e1(nc.sbuf_tensor("bdtb2", [128, KI], F32))
    bcw = e1(nc.sbuf_tensor("bcw", [128, KI, 4], F32))
    bcb = e1(nc.sbuf_tensor("bcb", [128, KI], F32))
    bdsk = e1(nc.sbuf_tensor("bdsk", [128, KI], F32))

    pA = e(nc.psum_tensor("pA", [128, TC], F32))
    pB = e(nc.psum_tensor("pB", [128, TC], F32))
    pXDB = e(nc.psum_tensor("pXDB", [128, TC], F32))
    pROW = e(nc.psum_tensor("pROW", [128, TC], F32))
    pRB0 = e(nc.psum_tensor("pRB0", [128, TC], F32))
    pRC = e(nc.psum_tensor("pRC", [128, TC], F32))

    sL = Sem(es, nc, "sL")
    sG = Sem(es, nc, "sG")
    sT = Sem(es, nc, "sT")
    sV = Sem(es, nc, "sV")
    sS = Sem(es, nc, "sS")
    sGc = Sem(es, nc, "sGc")
    sSt = Sem(es, nc, "sSt")

    OPS = {k: [] for k in ("sync", "gpsimd", "vector", "scalar", "tensor")}
    LBL = {k: [] for k in ("sync", "gpsimd", "vector", "scalar", "tensor")}
    import traceback

    ser_now = [False]
    import os as _os0
    ser_eng = _os0.environ.get("RSENG", "")

    def emit(eng, fn, waits=(), incs=()):
        LBL[eng].append(traceback.extract_stack(limit=3)[0].lineno)
        if serialize or (ser_now[0] and (not ser_eng or ser_eng == eng)):
            waits = list(waits) + [(s_, s_.n) for s_ in
                                   (sL, sG, sT, sV, sS, sGc, sSt)]
        wl = [(s.h, int(v)) for (s, v) in waits if v > 0]
        pl = [(s.h, k) for (s, k) in incs]
        for s, k in incs:
            s.n += k
        def run(h, wl=wl, pl=pl, fn=fn):
            for hh, v in wl:
                h.wait_ge(hh, v)
            ins = fn(h)
            for hh, k in pl:
                ins = ins.then_inc(hh, k)
        OPS[eng].append(run)

    CM = bcstMS[:, 0:128]
    CS = bcstMS[:, 128:256]
    NEG1F = bcstF[:, 2:3]
    ONE1 = bcstF[:, 0:1]
    ZERO1 = bcstF[:, 1:2]
    ONER = bcstF[0:1, 0:1]
    ZEROR = bcstF[0:1, 1:2]

    def g_dma(dst, src, incs=None):
        emit("gpsimd", lambda g, d=dst, s=src: g.dma_start(d, s),
             incs=incs if incs is not None else [(sG, 16)])

    g_dma(bcstR[:, :], cst)
    g_dma(bcstMS[:, :], cstMS)
    g_dma(bonesrow[:, :], onesrow_h)
    g_dma(beye[:, :], eyeh)
    g_dma(bxpj[:, :, :], w_xpT)
    g_dma(bdtp[:, :], w_dtT)
    g_dma(w_inB, w_inT)
    g_dma(w_outB, w_outT)
    CASTS_SMALL = sG.n
    for z in range(NZ):
        g_dma(waB[z], wah[z])
        g_dma(wbB[z], wbh[z])
    CASTS_DONE = sG.n

    xTr = xT.rearrange("(c p) t -> p c t", p=128)
    g_dma(bx[:, :, :], xTr[:, :, 0:TC])
    XLOAD = [sG.n]

    emit("gpsimd", lambda g: g.memzero(bcar[:, :, :]))
    emit("gpsimd", lambda g: g.memzero(bhalo[:, :, :]))
    emit("gpsimd", lambda g: g.memzero(bpool[:, :]), incs=[(sGc, 1)])
    GZERO = sGc.n

    def s_dma(dst, src):
        emit("sync", lambda sy, d=dst, s=src: sy.dma_start(d, s),
             incs=[(sL, 16)])

    s_dma(bcstF[:, :], cst)
    s_dma(bcw[:, :, :], convw)
    s_dma(bcb[:, :], convb)
    s_dma(bdtb2[:, :], dtb)
    s_dma(bp6[:, 0, 0:KI * NST], alog.rearrange("p a b -> p (a b)"))
    s_dma(bdsk[:, :], dsk)
    s_dma(bb1[:, :], b1c)
    s_dma(bninf[:, :], ninf)
    s_dma(bb2[:, :], b2r)
    RESIDENT = sL.n

    emit("vector", lambda v: v.tensor_scalar_mul(bdtb2[:, :], bdtb2[:, :], 2.0),
         waits=[(sL, RESIDENT)], incs=[(sV, 1)])
    emit("scalar", lambda s: s.activation(
        bA[:, :, :].rearrange("p a b -> p (a b)"), bp6[:, 0, 0:KI * NST],
        AF.Exp, bias=ZERO1, scale=ONE1), waits=[(sL, RESIDENT)], incs=[(sS, 1)])
    emit("vector", lambda v: v.tensor_scalar_mul(bA[:, :, :], bA[:, :, :], -1.0),
         waits=[(sS, sS.n)], incs=[(sV, 1)])
    PREP_V = sV.n
    p6_free_init_s = sS.n

    wr_free = [(sV, 0), (sV, 0)]
    wr_ctr = [0]

    bank_free = {id(p): (sV, 0) for p in (pA, pB, pXDB, pROW, pRB0, pRC)}
    p6_free = {i: (sV, 0) for i in range(3)}
    p6_free[0] = (sS, p6_free_init_s)

    def mm(p_ap, p_key, st_ap, mv_ap, start, stop, extra_waits=(), incs=()):
        w = list(extra_waits)
        if start:
            w.append(bank_free[p_key])
        emit("tensor", lambda t, o=p_ap, a=st_ap, b=mv_ap, s1=start, s2=stop:
             t.matmul(o, a, b, start=s1, stop=s2), waits=w, incs=incs)

    def vop(fn, waits=()):
        emit("vector", fn, waits=waits, incs=[(sV, 1)])
        return sV.n

    def sop(fn, waits=()):
        emit("scalar", fn, waits=waits, incs=[(sS, 1)])
        return sS.n

    sq_free = {0: (sT, 0), 1: (sT, 0)}

    def _ln(nch, src_f32, src_mov, ready_waits):
        for ch in range(nch):
            mm(pROW[:, :], id(pROW), CM, src_mov(ch),
               start=(ch == 0), stop=False, extra_waits=list(ready_waits))
        base_T = sT.n
        for ch in range(nch):
            sl = ch % 2
            w = list(ready_waits)
            if ch >= 2:
                w.append((sT, base_T + ch - 1))
            sqd = sop(lambda s, c=ch, q=sl: s.activation(
                bsq[:, q], src_f32(c), AF.Square, bias=ZERO1, scale=ONE1),
                waits=w)
            mm(pROW[:, :], id(pROW), CS, bsq[:, sl],
               start=False, stop=(ch == nch - 1),
               extra_waits=[(sS, sqd)], incs=[(sT, 1)])
        stats_T = sT.n
        inv = 1.0 / (nch * 128)
        vop(lambda v: v.tensor_scalar_mul(brow[:, 0], pROW[0:1, :], inv),
            waits=[(sT, stats_T)])
        vop(lambda v: v.tensor_scalar_mul(brow[:, 1], pROW[32:33, :], inv))
        vop(lambda v: v.tensor_mul(brow[:, 2], brow[:, 0], brow[:, 0]))
        vop(lambda v: v.tensor_sub(brow[:, 1], brow[:, 1], brow[:, 2]))
        rows_v = vop(lambda v: v.tensor_scalar_add(brow[:, 1], brow[:, 1], EPS))
        sqrt_s = sop(lambda s: s.activation(brow[:, 2], brow[:, 1], AF.Sqrt,
                                            bias=ZEROR, scale=ONER),
                     waits=[(sV, rows_v)])
        vop(lambda v: v.reciprocal(browR[:, 0], brow[:, 2]),
            waits=[(sS, sqrt_s)])
        rows_done = vop(lambda v: v.tensor_mul(
            browR[:, 1], brow[:, 0], browR[:, 0].bitcast(F32)))
        bank_free[id(pROW)] = (sV, rows_done)
        return rows_done

    def bcast_row(row_r_ap, waits):
        mm(pROW[:, :], id(pROW), bonesrow[:, :], row_r_ap, start=True,
           stop=True, extra_waits=waits, incs=[(sT, 1)])
        return sT.n

    def ln_apply(nch, rows_done, src_f32, emit_final, sub_out=None):
        repa = bcast_row(browR[:, 0], [(sV, rows_done)])
        mm(pRC[:, :], id(pRC), bonesrow[:, :], browR[:, 1], start=True,
           stop=True, extra_waits=[(sV, rows_done)], incs=[(sT, 1)])
        repb = sT.n
        for ch in range(nch):
            sl = ch % 2
            r = vop(lambda v, c=ch, q=sl: v.tensor_mul(
                bp6[:, q], src_f32(c), pROW[:, :]),
                waits=[(sT, repb), p6_free[sl]])
            if sub_out is not None:
                r = vop(lambda v, c=ch, q=sl, o=sub_out(ch): v.tensor_sub(
                    o, bp6[:, q], pRC[:, :]))
                p6_free[sl] = (sV, r)
            else:
                r = vop(lambda v, c=ch, q=sl: v.tensor_sub(
                    bp6[:, q], bp6[:, q], pRC[:, :]))
            if emit_final is not None:
                emit_final(ch, sl)
        done = sV.n
        bank_free[id(pROW)] = (sV, done)
        bank_free[id(pRC)] = (sV, done)
        return done

    for i in range(NT):
        t0 = i * TC
        xw = [(sG, XLOAD[i]), (sV, PREP_V), (sGc, GZERO), (sL, RESIDENT)]

        rows1 = _ln(KD, lambda c: bx[:, c].bitcast(F32), lambda c: bx[:, c], xw)

        xn1_done = ln_apply(KD, rows1, lambda c: bx[:, c].bitcast(F32),
                            None, sub_out=lambda c: bxn1[:, c])
        if dbg_heavy:
            emit("scalar", lambda s, tt=t0: s.dma_start(
                dbg_xn1[:, :, tt:tt + TC], bxn1[:, :, :]),
                waits=[(sV, xn1_done)], incs=[(sSt, 16)])

        conv_free = [(sV, 0), (sV, 0)]
        u_done = [0] * KI
        for cp in range(8):
            slot = wr_ctr[0] % 2
            wr_ctr[0] += 1
            emit("sync", lambda sy, sl=slot, c=cp: sy.dma_start(
                bwr[:, sl, 0:KD], w_inB[c].transpose([1, 0, 2, 3])),
                waits=[wr_free[slot], (sG, CASTS_SMALL)], incs=[(sL, 16)])
            rdy = sL.n
            for c2 in range(2):
                ch = cp * 2 + c2
                pp, pk = (pA, id(pA)) if ch % 2 == 0 else (pB, id(pB))
                for k in range(KD):
                    mm(pp[:, :], pk, bwr[:, slot, k, c2, :], bxn1[:, k],
                       start=(k == 0), stop=(k == KD - 1),
                       extra_waits=[(sL, rdy), (sV, xn1_done)],
                       incs=([(sT, 1)] if k == KD - 1 else []))
                mm_T = sT.n
                if c2 == 1:
                    wr_free[slot] = (sT, mm_T)
                csl = 0
                r = vop(lambda v, q=csl, p=pp: v.tensor_copy(
                    bconv[:, q, 4:4 + TC], p[:, :]),
                    waits=[(sT, mm_T), conv_free[csl]])
                bank_free[pk] = (sV, r)
                if dbg_heavy:
                    emit("scalar", lambda s, q=csl, c=ch, tt=t0: s.dma_start(
                        dbg_xh[:, c, tt:tt + TC], bconv[:, q, 4:4 + TC]),
                        waits=[(sV, sV.n)], incs=[(sSt, 16)])
                vop(lambda v, q=csl, c=ch: v.tensor_copy(
                    bconv[:, q, 1:4], bhalo[:, c, :]))
                osl = ch % 2
                vop(lambda v, q=csl, c=ch, o=osl: v.tensor_scalar(
                    bp6[:, o], bconv[:, q, 4:4 + TC], bcw[:, c, 3:4],
                    bcb[:, c:c + 1], ALU.mult, ALU.add),
                    waits=[p6_free[osl]])
                for j in range(3):
                    vop(lambda v, q=csl, c=ch, o=osl, jj=j:
                        v.scalar_tensor_tensor(
                            bp6[:, o], bconv[:, q, 1 + jj:1 + jj + TC],
                            bcw[:, c, jj:jj + 1], bp6[:, o],
                            ALU.mult, ALU.add))
                conv_free[csl] = (sV, sV.n + 1)
                conv_v = vop(lambda v, q=csl, c=ch: v.tensor_copy(
                    bhalo[:, c, :], bconv[:, q, 1 + TC:4 + TC]))
                u_done[ch] = sop(lambda s, c=ch, o=osl: s.activation(
                    bu[:, c].bitcast(F32R), bp6[:, o], AF.Silu,
                    bias=ZERO1, scale=ONE1), waits=[(sV, conv_v)])
                p6_free[osl] = (sS, u_done[ch])

        for k in range(KI):
            inc = [(sT, 1)] if k == KI - 1 else []
            mm(pXDB[:, :], id(pXDB), bxpj[:, k, :], bu[:, k].bitcast(F32R),
               start=(k == 0), stop=(k == KI - 1),
               extra_waits=[(sS, u_done[k])], incs=inc)
        xdb_T = sT.n
        vop(lambda v: v.tensor_copy(bdtr[:, :], pXDB[0:64, :]),
            waits=[(sT, xdb_T)])
        vop(lambda v: v.tensor_copy(bBC[:, :], pXDB[64:80, :]))
        bc_done = vop(lambda v: v.tensor_copy(bCsb[:, :], pXDB[96:112, :]))
        bank_free[id(pXDB)] = (sV, bc_done)

        delta_done = [0] * KI
        for ch in range(KI):
            pp, pk = (pA, id(pA)) if ch % 2 == 0 else (pB, id(pB))
            mm(pp[:, :], pk, bdtp[:, ch * 128:(ch + 1) * 128], bdtr[:, :],
               start=True, stop=True, extra_waits=[(sV, bc_done)],
               incs=[(sT, 1)])
            dt_T = sT.n
            d0_v = vop(lambda v, c=ch, p=pp: v.tensor_scalar(
                bdelta[:, c].bitcast(F32R), p[:, :], bdtb2[:, c:c + 1], None,
                ALU.add), waits=[(sT, dt_T)])
            bank_free[pk] = (sV, d0_v)
            a_s = sop(lambda s, c=ch: s.activation(
                bp6[:, 0], bdelta[:, c], AF.Abs, bias=ZERO1, scale=ONE1),
                waits=[(sV, d0_v), p6_free[0]])
            e_s = sop(lambda s: s.activation(bp6[:, 1], bp6[:, 0], AF.Exp,
                                             bias=ZERO1, scale=NEG1F),
                      waits=[p6_free[1]])
            p6_free[0] = (sS, e_s)
            p1_v = vop(lambda v: v.tensor_scalar_add(bp6[:, 1], bp6[:, 1], 1.0),
                       waits=[(sS, e_s)])
            l_s = sop(lambda s: s.activation(bp6[:, 0], bp6[:, 1], AF.Ln,
                                             bias=ZERO1, scale=ONE1),
                      waits=[(sV, p1_v), p6_free[0]])
            p6_free[1] = (sS, l_s)
            delta_done[ch] = vop(lambda v, c=ch: v.scalar_tensor_tensor(
                bdelta[:, c].bitcast(F32R), bdelta[:, c], 0.0, bp6[:, 0],
                ALU.max, ALU.add), waits=[(sS, l_s)])
            p6_free[0] = (sV, delta_done[ch])

        du_done = [0] * KI
        for ch in range(KI):
            vop(lambda v, c=ch: v.tensor_scalar(
                by[:, c], bu[:, c], bdsk[:, c:c + 1], None, ALU.mult),
                waits=[(sS, u_done[ch])])
            du_done[ch] = vop(lambda v, c=ch: v.tensor_mul(
                bu[:, c].bitcast(F32R), bdelta[:, c], bu[:, c]),
                waits=[(sV, delta_done[ch])])
        all_du = sV.n

        import os as _os2
        if _os2.environ.get("RSER") == "scan":
            ser_now[0] = True
        slot_da = (sV, 0)
        slot_dbu = [(sV, 0), (sV, 0)]
        slot_h = [(sGc, 0), (sGc, 0)]
        crep_free = (sV, 0)
        for n in range(NST):
            mm(pRB0[:, :], id(pRB0), beye[:, n:n + 1].broadcast_to([16, 128]),
               bBC[:, :], start=True, stop=True,
               extra_waits=[(sV, bc_done)], incs=[(sT, 1)])
            brep_T = sT.n
            mm(pRC[:, :], id(pRC), beye[:, n:n + 1].broadcast_to([16, 128]),
               bCsb[:, :], start=True, stop=True,
               extra_waits=[(sV, bc_done)], incs=[(sT, 1)])
            crep_T = sT.n
            crep_v = vop(lambda v: v.tensor_copy(bcrep[:, :], pRC[:, :]),
                         waits=[(sT, crep_T), crep_free])
            bank_free[id(pRC)] = (sV, crep_v)
            last_dbu = 0
            scan_of = {}

            def tail(k):
                q = k % 2
                vop(lambda v, qq=q, c=k, nn=n: v.tensor_copy(
                    bcar[:, c, nn:nn + 1], bhh[:, qq, TC - 1:TC]))
                emit("gpsimd", lambda g, qq=q: g.tensor_tensor(
                    bdbu[:, qq], bhh[:, qq], bcrep[:, :], ALU.mult),
                    waits=[(sV, sV.n)], incs=[(sGc, 1)])
                mult_g = sGc.n
                slot_h[q] = (sGc, mult_g)
                ya = vop(lambda v, c=k, qq=q: v.tensor_add(
                    by[:, c], by[:, c], bdbu[:, qq]),
                    waits=[(sGc, mult_g)])
                slot_dbu[q] = (sV, ya)

            for ch in range(KI):
                sl = ch % 2
                da_s = sop(lambda s, c=ch, nn=n: s.activation(
                    bda[:, 0], bdelta[:, c], AF.Exp, bias=ZERO1,
                    scale=bA[:, c, nn:nn + 1]),
                    waits=[(sV, delta_done[ch]), slot_da])
                vop(lambda v, c=ch, q=sl, p=pRB0: v.tensor_mul(
                    bdbu[:, q], bu[:, c], p[:, :]),
                    waits=[(sV, du_done[ch]), (sT, brep_T), slot_dbu[sl]])
                last_dbu = sV.n
                scan_v = vop(lambda v, q=sl, c=ch, nn=n: v.tensor_tensor_scan(
                    bhh[:, q], bda[:, 0], bdbu[:, q], bcar[:, c, nn:nn + 1],
                    ALU.mult, ALU.add),
                    waits=[(sS, da_s), slot_h[sl]])
                slot_da = (sV, scan_v)
                scan_of[ch] = scan_v
                if ch >= 1:
                    tail(ch - 1)
            vop(lambda v: v.tensor_copy(bp6[:, 2, 0:256], bp6[:, 2, 256:512]))
            tail(KI - 1)
            bank_free[id(pRB0)] = (sV, last_dbu)
            crep_free = (sGc, sGc.n)
        y_scan_done = sV.n
        if _os2.environ.get("RSER") == "scan":
            ser_now[0] = False

        if dbg_heavy:
            emit("scalar", lambda s, tt=t0: s.dma_start(
                dbg_u[:, :, tt:tt + TC], bu[:, :, :]),
                waits=[(sV, all_du)], incs=[(sSt, 16)])
            emit("scalar", lambda s, tt=t0: s.dma_start(
                dbg_dl[:, :, tt:tt + TC], bdelta[:, :, :]),
                waits=[(sV, all_du)], incs=[(sSt, 16)])
            emit("scalar", lambda s, tt=t0: s.dma_start(
                dbg_y[:, :, tt:tt + TC], by[:, :, :]),
                waits=[(sV, y_scan_done)], incs=[(sSt, 16)])

        for cp in range(8, 16):
            slot = wr_ctr[0] % 2
            wr_ctr[0] += 1
            emit("sync", lambda sy, sl=slot, c=cp: sy.dma_start(
                bwr[:, sl, 0:KD], w_inB[c].transpose([1, 0, 2, 3])),
                waits=[wr_free[slot], (sG, CASTS_SMALL)], incs=[(sL, 16)])
            rdy = sL.n
            for c2 in range(2):
                ch = (cp - 8) * 2 + c2
                pp, pk = (pA, id(pA)) if ch % 2 == 0 else (pB, id(pB))
                for k in range(KD):
                    mm(pp[:, :], pk, bwr[:, slot, k, c2, :], bxn1[:, k],
                       start=(k == 0), stop=(k == KD - 1),
                       extra_waits=[(sL, rdy)],
                       incs=([(sT, 1)] if k == KD - 1 else []))
                z_T = sT.n
                if c2 == 1:
                    wr_free[slot] = (sT, z_T)
                zsl = ch % 2
                z_s = sop(lambda s, q=zsl, p=pp: s.activation(
                    bp6[:, q], p[:, :], AF.Silu, bias=ZERO1, scale=ONE1),
                    waits=[(sT, z_T), p6_free[zsl]])
                r = vop(lambda v, c=ch, q=zsl: v.tensor_mul(
                    byb[:, c], by[:, c], bp6[:, q]),
                    waits=[(sS, z_s), (sV, y_scan_done)])
                p6_free[zsl] = (sV, r)
                bank_free[pk] = (sV, r)
        yb_done = sV.n

        for dp in range(4):
            slot = wr_ctr[0] % 2
            wr_ctr[0] += 1
            emit("sync", lambda sy, sl=slot, d=dp: sy.dma_start(
                bwr[:, sl], w_outB[d].transpose([1, 0, 2, 3])),
                waits=[wr_free[slot], (sG, CASTS_SMALL)], incs=[(sL, 16)])
            rdy = sL.n
            for d2 in range(2):
                dch = dp * 2 + d2
                pp, pk = (pA, id(pA)) if dch % 2 == 0 else (pB, id(pB))
                for k in range(KI):
                    mm(pp[:, :], pk, bwr[:, slot, k, d2, :], byb[:, k],
                       start=(k == 0), stop=(k == KI - 1),
                       extra_waits=[(sL, rdy), (sV, yb_done)],
                       incs=([(sT, 1)] if k == KI - 1 else []))
                op_T = sT.n
                if d2 == 1:
                    wr_free[slot] = (sT, op_T)
                r = vop(lambda v, c=dch, p=pp: v.tensor_copy(
                    bmo[:, c].bitcast(F32R), p[:, :]), waits=[(sT, op_T)])
                bank_free[pk] = (sV, r)
        mo_all = sV.n

        rows2 = _ln(KD, lambda c: bmo[:, c], lambda c: bmo[:, c].bitcast(F32R),
                    [(sV, mo_all)])

        def fin2(ch, sl):
            r = vop(lambda v, c=ch, q=sl: v.tensor_add(
                bx[:, c], bx[:, c].bitcast(F32), bp6[:, q]))
            p6_free[sl] = (sV, r)
        x2_done = ln_apply(KD, rows2, lambda c: bmo[:, c], fin2)
        emit("scalar", lambda s, tt=t0: s.dma_start(
            x2s[:, :, tt:tt + TC], bx[:, :, :].bitcast(F32)),
            waits=[(sV, x2_done)], incs=[(sSt, 16)])
        x2_store = sSt.n
        if debug:
            emit("scalar", lambda s, tt=t0: s.dma_start(
                dbg_x2[:, :, tt:tt + TC], bx[:, :, :].bitcast(F32)),
                waits=[(sV, x2_done)], incs=[(sSt, 16)])

        rows3 = _ln(KD, lambda c: bx[:, c].bitcast(F32), lambda c: bx[:, c],
                    [(sV, x2_done)])

        def fin3(ch, sl):
            xv = sV.n
            emit("scalar", lambda s, c=ch, q=sl, tt=t0: s.dma_start(
                xn3s[:, c, tt:tt + TC], bp6[:, q]),
                waits=[(sV, xv)], incs=[(sSt, 16)])
            stv = sSt.n
            vop(lambda v, q=sl, c=ch: v.reduce_sum(
                bred[:, c:c + 1], bp6[:, q], axis=AX.X))
            p6_free[sl] = (sSt, stv)
            xn3_stores.append(stv)
        xn3_stores = []
        ln3_done = ln_apply(KD, rows3, lambda c: bx[:, c].bitcast(F32), fin3)
        vop(lambda v: v.tensor_copy(bp6[:, 2, 0:256], bp6[:, 2, 256:512]))
        vop(lambda v: v.tensor_add(bpool[:, :], bpool[:, :], bred[:, :]))
        for sl in (0, 1):
            p6_free[sl] = (sSt, xn3_stores[-1])

        if i + 1 < NT:
            emit("gpsimd", lambda g, tt=(i + 1) * TC: g.dma_start(
                bx[:, :, :], xTr[:, :, tt:tt + TC]),
                waits=[(sV, ln3_done), (sSt, x2_store)], incs=[(sG, 16)])
            XLOAD.append(sG.n)

    s1.close()
    s2 = ExitStack()
    e2 = s2.enter_context
    bw1 = e2(nc.sbuf_tensor("bw1", [128, KD, RH], F32R))
    bw2 = e2(nc.sbuf_tensor("bw2", [128, 2, 8], F32R))
    brr = e2(nc.sbuf_tensor("brr", [1, 64], F32))
    bwrow = e2(nc.sbuf_tensor("bwrow", [1, 8], F32R))
    bmsk = e2(nc.sbuf_tensor("bmsk", [1, 8], mybir.dt.uint8))
    bhrt = e2(nc.sbuf_tensor("bhrt", [128, 2, 128], F32R))
    bhb = e2(nc.sbuf_tensor("bhb", [128, 2], F32))
    bxn3b = e2(nc.sbuf_tensor("bxn3b", [128, KD, TC], BF16))
    bxn3f = e2(nc.sbuf_tensor("bxn3f", [128, KD, TC], F32))
    bH = e2(nc.sbuf_tensor("bH", [128, KF, TC], BF16))
    bzac = e2(nc.sbuf_tensor("bzac", [128, KD, TC], F32))
    bwa = e2(nc.sbuf_tensor("bwa", [128, 2, KD, 2, 128], BF16))
    bwb = e2(nc.sbuf_tensor("bwb", [128, 2, KF, 2, 128], BF16))

    P1BAR = [(sV, sV.n), (sT, sT.n), (sS, sS.n), (sGc, sGc.n),
             (sL, sL.n), (sG, sG.n), (sSt, sSt.n)]

    import os as _os
    if _os.environ.get("RSER"):
        ser_now[0] = True
    emit("gpsimd", lambda g: g.dma_start(
        bw1[:, :, :], w1h.rearrange("(c p) e -> p c e", p=128)),
        waits=P1BAR, incs=[(sG, 16)])
    emit("gpsimd", lambda g: g.dma_start(
        bw2[:, :, :], w2h.rearrange("(c p) e -> p c e", p=128)),
        waits=P1BAR, incs=[(sG, 16)])
    emit("gpsimd", lambda g: g.dma_start(
        bhrt[:, :, :].rearrange("p a b -> p (a b)"), z2x128),
        waits=P1BAR, incs=[(sG, 16)])
    W12 = sG.n
    pool_v = vop(lambda v: v.tensor_scalar_mul(
        bpoolr[:, :], bpool[:, :], 1.0 / L), waits=P1BAR)
    for j in range(2):
        for k in range(KD):
            inc = [(sT, 1)] if (j == 1 and k == KD - 1) else []
            mm(pROW[:, j * 8:(j + 1) * 8], id(pROW),
               bw1[:, k, j * 128:(j + 1) * 128],
               bpoolr[:, k:k + 1].broadcast_to([128, 8]),
               start=(j == 0 and k == 0), stop=(k == KD - 1),
               extra_waits=[(sV, pool_v), (sG, W12)], incs=inc)
    rt_T = sT.n

    def vpad():
        return vop(lambda v: v.tensor_copy(bp6[:, 2, 0:256],
                                           bp6[:, 2, 256:512]))

    for j in range(2):
        vop(lambda v, jj=j: v.tensor_scalar(
            bhb[:, jj:jj + 1], pROW[:, jj * 8:jj * 8 + 1],
            bb1[:, jj:jj + 1], None, ALU.add), waits=[(sT, rt_T)])
    hb_v = vpad()
    h_s = sop(lambda s: s.activation(
        bhrt[:, :, 0], bhb[:, :], AF.Tanh, bias=ZERO1, scale=ONE1),
        waits=[(sV, hb_v), (sG, W12)])
    bank_free[id(pROW)] = (sS, h_s)
    for j in range(2):
        inc = [(sT, 1)] if j == 1 else []
        mm(pXDB[:, 0:8], id(pXDB), bhrt[:, j, :], bw2[:, j, :],
           start=(j == 0), stop=(j == 1), extra_waits=[(sS, h_s)], incs=inc)
    lg_T = sT.n
    vop(lambda v: v.tensor_add(brr[:, 0:8], pXDB[0:1, 0:8], bb2[:, :]),
        waits=[(sT, lg_T)])
    bank_free[id(pXDB)] = (sV, sV.n)
    vpad()
    vop(lambda v: v.reduce_max(brr[:, 8:9], brr[:, 0:8], axis=AX.X))
    vpad()
    vop(lambda v: v.tensor_sub(brr[:, 0:8], brr[:, 0:8],
                               brr[:, 8:9].broadcast_to([1, 8])))
    sm_v = vpad()
    sm_s = sop(lambda s: s.activation(brr[:, 16:24], brr[:, 0:8], AF.Exp,
                                      bias=ZEROR, scale=ONER),
               waits=[(sV, sm_v)])
    vop(lambda v: v.reduce_sum(brr[:, 8:9], brr[:, 16:24], axis=AX.X),
        waits=[(sS, sm_s)])
    vpad()
    vop(lambda v: v.reciprocal(brr[:, 9:10], brr[:, 8:9]))
    vpad()
    vop(lambda v: v.tensor_mul(brr[:, 16:24], brr[:, 16:24],
                               brr[:, 9:10].broadcast_to([1, 8])))
    vpad()
    vop(lambda v: v.tensor_copy(brr[:, 24:32], brr[:, 16:24]))
    vpad()
    for rnd in range(3):
        vop(lambda v: v.reduce_max(brr[:, 8:9], brr[:, 24:32], axis=AX.X))
        vpad()
        if rnd < 2:
            vop(lambda v: v.tensor_tensor(
                bmsk[:, :], brr[:, 24:32],
                brr[:, 8:9].broadcast_to([1, 8]), ALU.is_ge))
            vpad()
            vop(lambda v: v.select(brr[:, 24:32], bmsk[:, :],
                                   bninf[:, :], brr[:, 24:32]))
            vpad()
    vop(lambda v: v.tensor_tensor(brr[:, 32:40], brr[:, 16:24],
                                  brr[:, 8:9].broadcast_to([1, 8]), ALU.is_ge))
    vpad()
    vop(lambda v: v.tensor_mul(brr[:, 24:32], brr[:, 16:24], brr[:, 32:40]))
    vpad()
    vop(lambda v: v.reduce_sum(brr[:, 8:9], brr[:, 24:32], axis=AX.X))
    vpad()
    vop(lambda v: v.reciprocal(brr[:, 9:10], brr[:, 8:9]))
    vpad()
    vop(lambda v: v.tensor_mul(
        bwrow[:, :], brr[:, 24:32],
        brr[:, 9:10].broadcast_to([1, 8])))
    wrow_v = vpad()
    mm(pROW[:, 0:8], id(pROW), bonesrow[:, :], bwrow[:, :],
       start=True, stop=True, extra_waits=[(sV, wrow_v)], incs=[(sT, 1)])
    wz_T = sT.n
    vop(lambda v: v.tensor_copy(bwz[:, :], pROW[:, 0:8]),
        waits=[(sT, wz_T)])
    WZ = vpad()
    bank_free[id(pROW)] = (sV, WZ)
    if debug:
        emit("scalar", lambda s: s.dma_start(dbg_rr, brr[:, :]),
             waits=[(sV, WZ)], incs=[(sSt, 16)])
        emit("scalar", lambda s: s.dma_start(dbg_pool, bpoolr[:, :].bitcast(F32)),
             waits=[(sV, WZ)], incs=[(sSt, 16)])
        emit("scalar", lambda s: s.dma_start(
            dbg_h, bhrt[:, :, 0].bitcast(F32), allow_slow_non_contiguous=True),
             waits=[(sV, WZ)], incs=[(sSt, 16)])
        emit("scalar", lambda s: s.dma_start(dbg_w, bwz[:, :]),
             waits=[(sV, WZ)], incs=[(sSt, 16)])

    if _os.environ.get("RSER") == "router":
        ser_now[0] = False
    wa_free = [(sT, 0), (sT, 0)]
    wb_free = [(sT, 0), (sT, 0)]
    h_free_T = 0
    xn3_used = 0
    x2_used = 0
    wa_ctr = [0]
    wb_ctr = [0]
    for st in range(NT):
        t0 = st * TC
        bar = P1BAR if st == 0 else []
        emit("sync", lambda sy, tt=t0: sy.dma_start(
            bxn3f[:, :, :], xn3s[:, :, tt:tt + TC]),
            waits=list(bar) + [(sV, xn3_used)], incs=[(sL, 16)])
        xn3f_ld = sL.n
        emit("gpsimd", lambda g, tt=t0: g.dma_start(
            bx[:, :, :], x2s[:, :, tt:tt + TC]),
            waits=list(bar) + [(sV, x2_used)], incs=[(sG, 16)])
        x2_ld = sG.n
        for ch in range(KD):
            vop(lambda v, c=ch: v.tensor_copy(bxn3b[:, c], bxn3f[:, c]),
                waits=[(sL, xn3f_ld)])
        xn3b_v = sV.n

        for ez in range(NZ):
            h_done = [0] * KF
            wa_rdy = 0
            for fc in range(KF):
                if fc % 2 == 0:
                    slot = wa_ctr[0] % 2
                    wa_ctr[0] += 1
                    emit("sync", lambda sy, z=ez, f=fc, sl=slot: sy.dma_start(
                        bwa[:, sl], waB[z, f // 2].transpose([1, 0, 2, 3])),
                        waits=list(bar) + [wa_free[slot], (sG, CASTS_DONE)],
                        incs=[(sL, 16)])
                    wa_rdy = sL.n
                    cur_wa = slot
                pp, pk = (pA, id(pA)) if fc % 2 == 0 else (pB, id(pB))
                for k in range(KD):
                    mm(pp[:, :], pk, bwa[:, cur_wa, k, fc % 2, :], bxn3b[:, k],
                       start=(k == 0), stop=(k == KD - 1),
                       extra_waits=[(sL, wa_rdy), (sV, xn3b_v),
                                    (sT, h_free_T)],
                       incs=([(sT, 1)] if k == KD - 1 else []))
                hT = sT.n
                if fc % 2 == 1:
                    wa_free[cur_wa] = (sT, hT)
                h_done[fc] = sop(lambda s, f=fc, p=pp: s.activation(
                    bH[:, f], p[:, :], AF.Gelu_apprx_tanh,
                    bias=ZERO1, scale=ONE1), waits=[(sT, hT)])
                bank_free[pk] = (sS, h_done[fc])
            for dp in range(4):
                slot = wb_ctr[0] % 2
                wb_ctr[0] += 1
                emit("sync", lambda sy, z=ez, d=dp, sl=slot: sy.dma_start(
                    bwb[:, sl], wbB[z, d].transpose([1, 0, 2, 3])),
                    waits=list(bar) + [wb_free[slot], (sG, CASTS_DONE)],
                    incs=[(sL, 16)])
                wb_rdy = sL.n
                for d2 in range(2):
                    dch = dp * 2 + d2
                    pp, pk = (pRB0, id(pRB0)) if dch % 2 == 0 else (pRC, id(pRC))
                    for fc in range(KF):
                        mm(pp[:, :], pk, bwb[:, slot, fc, d2, :], bH[:, fc],
                           start=(fc == 0), stop=(fc == KF - 1),
                           extra_waits=[(sL, wb_rdy), (sS, h_done[fc])],
                           incs=([(sT, 1)] if fc == KF - 1 else []))
                    zT = sT.n
                    if d2 == 1:
                        wb_free[slot] = (sT, zT)
                    if ez == 0:
                        r = vop(lambda v, c=dch, p=pp: v.scalar_tensor_tensor(
                            bzac[:, c].bitcast(F32R), p[:, :], bwz[:, 0:1],
                            bxn3f[:, c], ALU.mult, ALU.add),
                            waits=[(sT, zT), (sV, WZ), (sL, xn3f_ld)])
                    else:
                        r = vop(lambda v, c=dch, p=pp, z=ez:
                                v.scalar_tensor_tensor(
                                    bzac[:, c].bitcast(F32R), p[:, :],
                                    bwz[:, z:z + 1], bzac[:, c],
                                    ALU.mult, ALU.add), waits=[(sT, zT)])
                    bank_free[pk] = (sV, r)
            h_free_T = sT.n
        xn3_used = sV.n

        f_done = sV.n

        rows4 = _ln(KD, lambda c: bzac[:, c],
                    lambda c: bzac[:, c].bitcast(F32R), [(sV, f_done)])

        def fin4(ch, sl):
            r = vop(lambda v, c=ch, q=sl: v.tensor_add(
                bx[:, c], bx[:, c].bitcast(F32), bp6[:, q]),
                waits=[(sG, x2_ld)])
            p6_free[sl] = (sV, r)
        out_v = ln_apply(KD, rows4, lambda c: bzac[:, c], fin4)
        x2_used = out_v
        emit("scalar", lambda s, tt=t0: s.dma_start(
            out[:, :, tt:tt + TC], bx[:, :, :].bitcast(F32)),
            waits=[(sV, out_v)], incs=[(sSt, 16)])

    FIN = sSt.n

    def fin_wait(h):
        h.wait_ge(sSt.h, FIN)
    OPS["scalar"].append(fin_wait)

    s2.close()
    build.LBL = LBL
    with nc.allow_low_precision(reason="bf16/f32r matmul operands; fp32 accumulation everywhere"), nc.Block() as blk:
        @blk.sync
        def _(h):
            for f in OPS["sync"]:
                f(h)

        @blk.gpsimd
        def _(h):
            for f in OPS["gpsimd"]:
                f(h)

        @blk.tensor
        def _(h):
            for f in OPS["tensor"]:
                f(h)

        @blk.vector
        def _(h):
            for f in OPS["vector"]:
                f(h)

        @blk.scalar
        def _(h):
            for f in OPS["scalar"]:
                f(h)

    es.close()
    return nc


def _prep_weights(inp):
    f32 = np.float32
    w = {}
    ipw = np.asarray(inp["L0_attn_in_proj_w"], f32)
    w_inT = np.ascontiguousarray(ipw.T)
    w["w_inT"] = np.ascontiguousarray(
        w_inT.reshape(KD, 128, 16, 2, 128).transpose(2, 0, 1, 3, 4))
    opw = np.asarray(inp["L0_attn_out_proj_w"], f32)
    w_outT = np.ascontiguousarray(opw.T)
    w["w_outT"] = np.ascontiguousarray(
        w_outT.reshape(KI, 128, 4, 2, 128).transpose(2, 0, 1, 3, 4))
    xp = np.asarray(inp["L0_attn_x_proj_w"], f32).T
    xpp = np.zeros((DI, 128), f32)
    xpp[:, 0:64] = xp[:, 0:64]
    xpp[:, 64:80] = xp[:, 64:80]
    xpp[:, 96:112] = xp[:, 80:96]
    w["w_xpT"] = np.ascontiguousarray(
        xpp.reshape(KI, 128, 128).transpose(1, 0, 2))
    w["w_dtT"] = np.ascontiguousarray(np.asarray(inp["L0_attn_dt_proj_w"], f32).T)
    w["convw"] = np.ascontiguousarray(
        np.asarray(inp["L0_attn_conv_w"], f32)[:, 0, :]
        .reshape(KI, 128, 4).transpose(1, 0, 2))
    w["convb"] = np.ascontiguousarray(
        np.asarray(inp["L0_attn_conv_b"], f32).reshape(KI, 128).T)
    w["dtb"] = np.ascontiguousarray(
        np.asarray(inp["L0_attn_dt_bias"], f32).reshape(KI, 128).T)
    w["alog"] = np.ascontiguousarray(
        np.asarray(inp["L0_attn_A_log"], f32).reshape(KI, 128, NST)
        .transpose(1, 0, 2))
    w["dsk"] = np.ascontiguousarray(
        np.asarray(inp["L0_attn_D_skip"], f32).reshape(KI, 128).T)
    w["w1h"] = np.ascontiguousarray(np.asarray(inp["L0_ffn_W1"], f32))
    w["b1c"] = np.ascontiguousarray(
        np.asarray(inp["L0_ffn_b1"], f32).reshape(2, 128).T)
    w["w2h"] = np.ascontiguousarray(np.asarray(inp["L0_ffn_W2"], f32))
    w["b2r"] = np.ascontiguousarray(np.asarray(inp["L0_ffn_b2"], f32).reshape(1, 8))
    Wa = np.asarray(inp["L0_ffn_Wa"], f32)
    w["wah"] = np.ascontiguousarray(
        Wa.reshape(NZ, KD, 128, 16, 2, 128).transpose(0, 3, 1, 2, 4, 5))
    Wb = np.asarray(inp["L0_ffn_Wb"], f32)
    w["wbh"] = np.ascontiguousarray(
        Wb.reshape(NZ, KF, 128, 4, 2, 128).transpose(0, 3, 1, 2, 4, 5))
    cstc = np.zeros((128, 8), f32)
    cstc[:, 0] = 1.0
    cstc[:, 2] = -1.0
    cstc[:, 5] = 1.0
    w["cst"] = cstc
    cms = np.zeros((128, 256), f32)
    cms[:, 0] = 1.0
    cms[:, 128 + 32] = 1.0
    w["cstMS"] = cms
    w["z2x128"] = np.zeros((128, 256), f32)
    w["eyeh"] = np.eye(16, dtype=f32)
    w["onesrow"] = np.ones((1, 128), f32)
    w["ninf"] = np.full((1, 8), -1e30, f32)
    return w


def kernel(**inputs):
    x = np.asarray(inputs["L0_attn_hidden_states"], np.float32)
    B, L, _ = x.shape
    wts = _prep_weights(inputs)
    nc = build(L)
    in_maps = []
    for b in range(B):
        m = dict(wts)
        m["xT"] = np.ascontiguousarray(x[b].T)
        in_maps.append(m)
    res = run_bass_kernel_spmd(nc, in_maps, core_ids=list(range(B)))
    outs = []
    for r in res.results:
        o = np.asarray(r["out"]).reshape(128, KD, L)
        od = np.ascontiguousarray(o.transpose(1, 0, 2).reshape(D, L).T)
        outs.append(od)
    return np.stack(outs).astype(np.float32)


# version 03
# speedup vs baseline: 1.3652x; 1.3652x over previous
import numpy as np
from contextlib import ExitStack

import concourse.bass as bass
import concourse.mybir as mybir
from concourse.bass_utils import run_bass_kernel_spmd

F32 = mybir.dt.float32
F32R = mybir.dt.float32r
BF16 = mybir.dt.bfloat16
AF = mybir.ActivationFunctionType
ALU = mybir.AluOpType
AX = mybir.AxisListType

D = 1024
DI = 2048
NST = 16
RK = 64
KD = D // 128
KI = DI // 128
TC = 512
NZ = 8
DFF = 4096
KF = DFF // 128
RH = 256
EPS = 1e-5


class Sem:
    def __init__(self, es, nc, name):
        self.h = es.enter_context(nc.semaphore(name))
        self.n = 0


def build(L, debug=False, serialize=False):
    import os as _osd
    dbg_heavy = debug and not _osd.environ.get("NOHEAVY")
    nc = bass.Bass("TRN2", target_bir_lowering=False)
    NT = L // TC

    xT = nc.dram_tensor("xT", [D, L], F32, kind="ExternalInput").ap()
    w_inT = nc.dram_tensor("w_inT", [16, KD, 128, 2, 128], F32, kind="ExternalInput").ap()
    w_outT = nc.dram_tensor("w_outT", [4, KI, 128, 2, 128], F32, kind="ExternalInput").ap()
    w_xpT = nc.dram_tensor("w_xpT", [128, KI, 128], F32, kind="ExternalInput").ap()
    w_dtT = nc.dram_tensor("w_dtT", [RK, DI], F32, kind="ExternalInput").ap()
    convw = nc.dram_tensor("convw", [128, KI, 4], F32, kind="ExternalInput").ap()
    convb = nc.dram_tensor("convb", [128, KI], F32, kind="ExternalInput").ap()
    dtb = nc.dram_tensor("dtb", [128, KI], F32, kind="ExternalInput").ap()
    alog = nc.dram_tensor("alog", [128, KI, NST], F32, kind="ExternalInput").ap()
    dsk = nc.dram_tensor("dsk", [128, KI], F32, kind="ExternalInput").ap()
    w1h = nc.dram_tensor("w1h", [D, RH], F32, kind="ExternalInput").ap()
    b1c = nc.dram_tensor("b1c", [128, 2], F32, kind="ExternalInput").ap()
    w2h = nc.dram_tensor("w2h", [RH, 8], F32, kind="ExternalInput").ap()
    b2r = nc.dram_tensor("b2r", [1, 8], F32, kind="ExternalInput").ap()
    wah = nc.dram_tensor("wah", [NZ, 16, KD, 128, 2, 128], F32, kind="ExternalInput").ap()
    wbh = nc.dram_tensor("wbh", [NZ, 4, KF, 128, 2, 128], F32, kind="ExternalInput").ap()
    cst = nc.dram_tensor("cst", [128, 8], F32, kind="ExternalInput").ap()
    cstMS = nc.dram_tensor("cstMS", [128, 256], F32, kind="ExternalInput").ap()
    z2x128 = nc.dram_tensor("z2x128", [128, 256], F32, kind="ExternalInput").ap()
    eyeh = nc.dram_tensor("eyeh", [16, NST], F32, kind="ExternalInput").ap()
    ninf = nc.dram_tensor("ninf", [1, 8], F32, kind="ExternalInput").ap()
    onesrow_h = nc.dram_tensor("onesrow", [1, 128], F32, kind="ExternalInput").ap()

    w_inB = nc.dram_tensor("w_inB", [16, KD, 128, 2, 128], BF16).ap()
    w_outB = nc.dram_tensor("w_outB", [4, KI, 128, 2, 128], BF16).ap()
    waB = nc.dram_tensor("waB", [NZ, 16, KD, 128, 2, 128], BF16).ap()
    wbB = nc.dram_tensor("wbB", [NZ, 4, KF, 128, 2, 128], BF16).ap()
    if debug:
        x2s = nc.dram_tensor("x2s", [128, KD, L], F32, kind="ExternalOutput").ap()
        xn3s = nc.dram_tensor("xn3s", [128, KD, L], F32, kind="ExternalOutput").ap()
    else:
        x2s = nc.dram_tensor("x2s", [128, KD, L], F32).ap()
        xn3s = nc.dram_tensor("xn3s", [128, KD, L], F32).ap()
    out = nc.dram_tensor("out", [128, KD, L], F32, kind="ExternalOutput").ap()
    if debug:
        dbg_u = nc.dram_tensor("dbg_u", [128, KI, L], F32, kind="ExternalOutput").ap()
        dbg_dl = nc.dram_tensor("dbg_dl", [128, KI, L], F32, kind="ExternalOutput").ap()
        dbg_y = nc.dram_tensor("dbg_y", [128, KI, L], F32, kind="ExternalOutput").ap()
        dbg_w = nc.dram_tensor("dbg_w", [128, 8], F32, kind="ExternalOutput").ap()
        dbg_x2 = nc.dram_tensor("dbg_x2", [128, KD, L], F32, kind="ExternalOutput").ap()
        dbg_xn1 = nc.dram_tensor("dbg_xn1", [128, KD, L], BF16, kind="ExternalOutput").ap()
        dbg_rr = nc.dram_tensor("dbg_rr", [1, 64], F32, kind="ExternalOutput").ap()
        dbg_pool = nc.dram_tensor("dbg_pool", [128, KD], F32, kind="ExternalOutput").ap()
        dbg_h = nc.dram_tensor("dbg_h", [128, 2], F32, kind="ExternalOutput").ap()
        dbg_xh = nc.dram_tensor("dbg_xh", [128, KI, L], F32, kind="ExternalOutput").ap()

    es = ExitStack()
    e = es.enter_context
    s1 = ExitStack()
    e1 = s1.enter_context

    bx = e(nc.sbuf_tensor("bx", [128, KD, TC], F32R))
    bp6 = e(nc.sbuf_tensor("bp6", [128, 3, TC], F32))
    brow = e(nc.sbuf_tensor("brow", [1, 3, TC], F32))
    browR = e(nc.sbuf_tensor("browR", [1, 2, TC], F32R))
    bsq = e(nc.sbuf_tensor("bsq", [128, 2, TC], F32R))
    bred = e(nc.sbuf_tensor("bred", [128, KD], F32))
    bpool = e(nc.sbuf_tensor("bpool", [128, KD], F32))
    bcstF = e(nc.sbuf_tensor("bcstF", [128, 8], F32))
    bcstR = e(nc.sbuf_tensor("bcstR", [128, 8], F32R))
    bcstMS = e(nc.sbuf_tensor("bcstMS", [128, 256], F32R))
    bonesrow = e(nc.sbuf_tensor("bonesrow", [1, 128], F32R))
    bninf = e(nc.sbuf_tensor("bninf", [1, 8], F32))
    bb2 = e(nc.sbuf_tensor("bb2", [1, 8], F32))
    bb1 = e(nc.sbuf_tensor("bb1", [128, 2], F32))
    bwz = e(nc.sbuf_tensor("bwz", [128, 8], F32))
    bpoolr = e(nc.sbuf_tensor("bpoolr", [128, KD], F32R))

    bxn1 = e1(nc.sbuf_tensor("bxn1", [128, KD, TC], BF16))
    bu = e1(nc.sbuf_tensor("bu", [128, KI, TC], F32))
    bdelta = e1(nc.sbuf_tensor("bdelta", [128, KI, TC], F32))
    by = e1(nc.sbuf_tensor("by", [128, KI, TC], F32))
    byb = e1(nc.sbuf_tensor("byb", [128, KI, TC], BF16))
    bmo = bdelta[:, 0:KD, :]
    bconv = e1(nc.sbuf_tensor("bconv", [128, 1, TC + 4], F32))
    bhalo = e1(nc.sbuf_tensor("bhalo", [128, KI, 3], F32))
    bda = e1(nc.sbuf_tensor("bda", [128, 2, TC], F32))
    bdbu = e1(nc.sbuf_tensor("bdbu", [128, 2, TC], F32))
    bhh = e1(nc.sbuf_tensor("bhh", [128, 2, TC], F32))
    bcrep = e1(nc.sbuf_tensor("bcrep", [128, TC], F32))
    bcar = e1(nc.sbuf_tensor("bcar", [128, KI, NST], F32))
    bdtr = e1(nc.sbuf_tensor("bdtr", [64, TC], BF16))
    bBC = e1(nc.sbuf_tensor("bBC", [16, TC], BF16))
    bCsb = e1(nc.sbuf_tensor("bCsb", [16, TC], BF16))
    beye = e1(nc.sbuf_tensor("beye", [16, NST], BF16))
    bwr = e1(nc.sbuf_tensor("bwr", [128, 2, KI, 2, 128], BF16))
    bxpj = e1(nc.sbuf_tensor("bxpj", [128, KI, 128], F32R))
    bdtp = e1(nc.sbuf_tensor("bdtp", [64, DI], BF16))
    bA = e1(nc.sbuf_tensor("bA", [128, KI, NST], F32))
    bdtb2 = e1(nc.sbuf_tensor("bdtb2", [128, KI], F32))
    bcw = e1(nc.sbuf_tensor("bcw", [128, KI, 4], F32))
    bcb = e1(nc.sbuf_tensor("bcb", [128, KI], F32))
    bdsk = e1(nc.sbuf_tensor("bdsk", [128, KI], F32))

    pA = e(nc.psum_tensor("pA", [128, TC], F32))
    pB = e(nc.psum_tensor("pB", [128, TC], F32))
    pXDB = e(nc.psum_tensor("pXDB", [128, TC], F32))
    pROW = e(nc.psum_tensor("pROW", [128, TC], F32))
    pRB0 = e(nc.psum_tensor("pRB0", [128, TC], F32))
    pRC = e(nc.psum_tensor("pRC", [128, TC], F32))

    sL = Sem(es, nc, "sL")
    sG = Sem(es, nc, "sG")
    sT = Sem(es, nc, "sT")
    sV = Sem(es, nc, "sV")
    sS = Sem(es, nc, "sS")
    sGc = Sem(es, nc, "sGc")
    sSt = Sem(es, nc, "sSt")

    OPS = {k: [] for k in ("sync", "gpsimd", "vector", "scalar", "tensor")}
    LBL = {k: [] for k in ("sync", "gpsimd", "vector", "scalar", "tensor")}
    import traceback

    ser_now = [False]
    import os as _os0
    ser_eng = _os0.environ.get("RSENG", "")

    def emit(eng, fn, waits=(), incs=()):
        LBL[eng].append(traceback.extract_stack(limit=3)[0].lineno)
        if serialize or (ser_now[0] and (not ser_eng or ser_eng == eng)):
            waits = list(waits) + [(s_, s_.n) for s_ in
                                   (sL, sG, sT, sV, sS, sGc, sSt)]
        wl = [(s.h, int(v)) for (s, v) in waits if v > 0]
        pl = [(s.h, k) for (s, k) in incs]
        for s, k in incs:
            s.n += k
        def run(h, wl=wl, pl=pl, fn=fn):
            for hh, v in wl:
                h.wait_ge(hh, v)
            ins = fn(h)
            for hh, k in pl:
                ins = ins.then_inc(hh, k)
        OPS[eng].append(run)

    CM = bcstMS[:, 0:128]
    CS = bcstMS[:, 128:256]
    NEG1F = bcstF[:, 2:3]
    ONE1 = bcstF[:, 0:1]
    ZERO1 = bcstF[:, 1:2]
    ONER = bcstF[0:1, 0:1]
    ZEROR = bcstF[0:1, 1:2]

    def g_dma(dst, src, incs=None):
        emit("gpsimd", lambda g, d=dst, s=src: g.dma_start(d, s),
             incs=incs if incs is not None else [(sG, 16)])

    g_dma(bcstR[:, :], cst)
    g_dma(bcstMS[:, :], cstMS)
    g_dma(bonesrow[:, :], onesrow_h)
    g_dma(beye[:, :], eyeh)
    g_dma(bxpj[:, :, :], w_xpT)
    g_dma(bdtp[:, :], w_dtT)
    g_dma(w_inB, w_inT)
    g_dma(w_outB, w_outT)
    CASTS_SMALL = sG.n
    for z in range(NZ):
        g_dma(waB[z], wah[z])
        g_dma(wbB[z], wbh[z])
    CASTS_DONE = sG.n

    xTr = xT.rearrange("(c p) t -> p c t", p=128)
    g_dma(bx[:, :, :], xTr[:, :, 0:TC])
    XLOAD = [sG.n]

    emit("gpsimd", lambda g: g.memzero(bcar[:, :, :]))
    emit("gpsimd", lambda g: g.memzero(bhalo[:, :, :]))
    emit("gpsimd", lambda g: g.memzero(bpool[:, :]), incs=[(sGc, 1)])
    GZERO = sGc.n

    def s_dma(dst, src):
        emit("sync", lambda sy, d=dst, s=src: sy.dma_start(d, s),
             incs=[(sL, 16)])

    s_dma(bcstF[:, :], cst)
    s_dma(bcw[:, :, :], convw)
    s_dma(bcb[:, :], convb)
    s_dma(bdtb2[:, :], dtb)
    s_dma(bp6[:, 0, 0:KI * NST], alog.rearrange("p a b -> p (a b)"))
    s_dma(bdsk[:, :], dsk)
    s_dma(bb1[:, :], b1c)
    s_dma(bninf[:, :], ninf)
    s_dma(bb2[:, :], b2r)
    RESIDENT = sL.n

    emit("vector", lambda v: v.tensor_scalar_mul(bdtb2[:, :], bdtb2[:, :], 2.0),
         waits=[(sL, RESIDENT)], incs=[(sV, 1)])
    emit("scalar", lambda s: s.activation(
        bA[:, :, :].rearrange("p a b -> p (a b)"), bp6[:, 0, 0:KI * NST],
        AF.Exp, bias=ZERO1, scale=ONE1), waits=[(sL, RESIDENT)], incs=[(sS, 1)])
    emit("vector", lambda v: v.tensor_scalar_mul(bA[:, :, :], bA[:, :, :], -1.0),
         waits=[(sS, sS.n)], incs=[(sV, 1)])
    PREP_V = sV.n
    p6_free_init_s = sS.n

    wr_free = [(sV, 0), (sV, 0)]
    wr_ctr = [0]

    bank_free = {id(p): (sV, 0) for p in (pA, pB, pXDB, pROW, pRB0, pRC)}
    p6_free = {i: (sV, 0) for i in range(3)}
    p6_free[0] = (sS, p6_free_init_s)

    def mm(p_ap, p_key, st_ap, mv_ap, start, stop, extra_waits=(), incs=()):
        w = list(extra_waits)
        if start:
            w.append(bank_free[p_key])
        emit("tensor", lambda t, o=p_ap, a=st_ap, b=mv_ap, s1=start, s2=stop:
             t.matmul(o, a, b, start=s1, stop=s2), waits=w, incs=incs)

    def vop(fn, waits=()):
        emit("vector", fn, waits=waits, incs=[(sV, 1)])
        return sV.n

    def sop(fn, waits=()):
        emit("scalar", fn, waits=waits, incs=[(sS, 1)])
        return sS.n

    sq_free = {0: (sT, 0), 1: (sT, 0)}

    def _ln(nch, src_f32, src_mov, ready_waits):
        for ch in range(nch):
            mm(pROW[:, :], id(pROW), CM, src_mov(ch),
               start=(ch == 0), stop=False, extra_waits=list(ready_waits))
        base_T = sT.n
        for ch in range(nch):
            sl = ch % 2
            w = list(ready_waits)
            if ch >= 2:
                w.append((sT, base_T + ch - 1))
            sqd = sop(lambda s, c=ch, q=sl: s.activation(
                bsq[:, q], src_f32(c), AF.Square, bias=ZERO1, scale=ONE1),
                waits=w)
            mm(pROW[:, :], id(pROW), CS, bsq[:, sl],
               start=False, stop=(ch == nch - 1),
               extra_waits=[(sS, sqd)], incs=[(sT, 1)])
        stats_T = sT.n
        inv = 1.0 / (nch * 128)
        vop(lambda v: v.tensor_scalar_mul(brow[:, 0], pROW[0:1, :], inv),
            waits=[(sT, stats_T)])
        vop(lambda v: v.tensor_scalar_mul(brow[:, 1], pROW[32:33, :], inv))
        vop(lambda v: v.tensor_mul(brow[:, 2], brow[:, 0], brow[:, 0]))
        vop(lambda v: v.tensor_sub(brow[:, 1], brow[:, 1], brow[:, 2]))
        rows_v = vop(lambda v: v.tensor_scalar_add(brow[:, 1], brow[:, 1], EPS))
        sqrt_s = sop(lambda s: s.activation(brow[:, 2], brow[:, 1], AF.Sqrt,
                                            bias=ZEROR, scale=ONER),
                     waits=[(sV, rows_v)])
        vop(lambda v: v.reciprocal(browR[:, 0], brow[:, 2]),
            waits=[(sS, sqrt_s)])
        rows_done = vop(lambda v: v.tensor_mul(
            browR[:, 1], brow[:, 0], browR[:, 0].bitcast(F32)))
        bank_free[id(pROW)] = (sV, rows_done)
        return rows_done

    def bcast_row(row_r_ap, waits):
        mm(pROW[:, :], id(pROW), bonesrow[:, :], row_r_ap, start=True,
           stop=True, extra_waits=waits, incs=[(sT, 1)])
        return sT.n

    def ln_apply(nch, rows_done, src_f32, emit_final, sub_out=None):
        repa = bcast_row(browR[:, 0], [(sV, rows_done)])
        mm(pRC[:, :], id(pRC), bonesrow[:, :], browR[:, 1], start=True,
           stop=True, extra_waits=[(sV, rows_done)], incs=[(sT, 1)])
        repb = sT.n
        for ch in range(nch):
            sl = ch % 2
            r = vop(lambda v, c=ch, q=sl: v.tensor_mul(
                bp6[:, q], src_f32(c), pROW[:, :]),
                waits=[(sT, repb), p6_free[sl]])
            if sub_out is not None:
                r = vop(lambda v, c=ch, q=sl, o=sub_out(ch): v.tensor_sub(
                    o, bp6[:, q], pRC[:, :]))
                p6_free[sl] = (sV, r)
            else:
                r = vop(lambda v, c=ch, q=sl: v.tensor_sub(
                    bp6[:, q], bp6[:, q], pRC[:, :]))
            if emit_final is not None:
                emit_final(ch, sl)
        done = sV.n
        bank_free[id(pROW)] = (sV, done)
        bank_free[id(pRC)] = (sV, done)
        return done

    import os as _osp
    SKIP1 = bool(_osp.environ.get("SKIP_P1"))
    SKIP2 = bool(_osp.environ.get("SKIP_P2"))
    for i in range(NT if not SKIP1 else 1):
        t0 = i * TC
        xw = [(sG, XLOAD[i]), (sV, PREP_V), (sGc, GZERO), (sL, RESIDENT)]

        rows1 = _ln(KD, lambda c: bx[:, c].bitcast(F32), lambda c: bx[:, c], xw)

        xn1_done = ln_apply(KD, rows1, lambda c: bx[:, c].bitcast(F32),
                            None, sub_out=lambda c: bxn1[:, c])
        if dbg_heavy:
            emit("scalar", lambda s, tt=t0: s.dma_start(
                dbg_xn1[:, :, tt:tt + TC], bxn1[:, :, :]),
                waits=[(sV, xn1_done)], incs=[(sSt, 16)])

        conv_free = [(sV, 0), (sV, 0)]
        u_done = [0] * KI
        for cp in range(8):
            slot = wr_ctr[0] % 2
            wr_ctr[0] += 1
            emit("sync", lambda sy, sl=slot, c=cp: sy.dma_start(
                bwr[:, sl, 0:KD], w_inB[c].transpose([1, 0, 2, 3])),
                waits=[wr_free[slot], (sG, CASTS_SMALL)], incs=[(sL, 16)])
            rdy = sL.n
            for c2 in range(2):
                ch = cp * 2 + c2
                pp, pk = (pA, id(pA)) if ch % 2 == 0 else (pB, id(pB))
                for k in range(KD):
                    w0 = ([(sL, rdy), (sV, xn1_done)] if k == 0 else [])
                    mm(pp[:, :], pk, bwr[:, slot, k, c2, :], bxn1[:, k],
                       start=(k == 0), stop=(k == KD - 1),
                       extra_waits=w0,
                       incs=([(sT, 1)] if k == KD - 1 else []))
                mm_T = sT.n
                if c2 == 1:
                    wr_free[slot] = (sT, mm_T)
                csl = 0
                r = vop(lambda v, q=csl, p=pp: v.tensor_copy(
                    bconv[:, q, 4:4 + TC], p[:, :]),
                    waits=[(sT, mm_T), conv_free[csl]])
                bank_free[pk] = (sV, r)
                if dbg_heavy:
                    emit("scalar", lambda s, q=csl, c=ch, tt=t0: s.dma_start(
                        dbg_xh[:, c, tt:tt + TC], bconv[:, q, 4:4 + TC]),
                        waits=[(sV, sV.n)], incs=[(sSt, 16)])
                vop(lambda v, q=csl, c=ch: v.tensor_copy(
                    bconv[:, q, 1:4], bhalo[:, c, :]))
                osl = ch % 2
                vop(lambda v, q=csl, c=ch, o=osl: v.tensor_scalar(
                    bp6[:, o], bconv[:, q, 4:4 + TC], bcw[:, c, 3:4],
                    bcb[:, c:c + 1], ALU.mult, ALU.add),
                    waits=[p6_free[osl]])
                for j in range(3):
                    vop(lambda v, q=csl, c=ch, o=osl, jj=j:
                        v.scalar_tensor_tensor(
                            bp6[:, o], bconv[:, q, 1 + jj:1 + jj + TC],
                            bcw[:, c, jj:jj + 1], bp6[:, o],
                            ALU.mult, ALU.add))
                conv_free[csl] = (sV, sV.n + 1)
                conv_v = vop(lambda v, q=csl, c=ch: v.tensor_copy(
                    bhalo[:, c, :], bconv[:, q, 1 + TC:4 + TC]))
                u_done[ch] = sop(lambda s, c=ch, o=osl: s.activation(
                    bu[:, c].bitcast(F32R), bp6[:, o], AF.Silu,
                    bias=ZERO1, scale=ONE1), waits=[(sV, conv_v)])
                p6_free[osl] = (sS, u_done[ch])

        for k in range(KI):
            inc = [(sT, 1)] if k == KI - 1 else []
            mm(pXDB[:, :], id(pXDB), bxpj[:, k, :], bu[:, k].bitcast(F32R),
               start=(k == 0), stop=(k == KI - 1),
               extra_waits=[(sS, u_done[k])], incs=inc)
        xdb_T = sT.n
        vop(lambda v: v.tensor_copy(bdtr[:, :], pXDB[0:64, :]),
            waits=[(sT, xdb_T)])
        vop(lambda v: v.tensor_copy(bBC[:, :], pXDB[64:80, :]))
        bc_done = vop(lambda v: v.tensor_copy(bCsb[:, :], pXDB[96:112, :]))
        bank_free[id(pXDB)] = (sV, bc_done)

        delta_done = [0] * KI
        for ch in range(KI):
            pp, pk = (pA, id(pA)) if ch % 2 == 0 else (pB, id(pB))
            mm(pp[:, :], pk, bdtp[:, ch * 128:(ch + 1) * 128], bdtr[:, :],
               start=True, stop=True, extra_waits=[(sV, bc_done)],
               incs=[(sT, 1)])
            dt_T = sT.n
            d0_v = vop(lambda v, c=ch, p=pp: v.tensor_scalar(
                bdelta[:, c].bitcast(F32R), p[:, :], bdtb2[:, c:c + 1], None,
                ALU.add), waits=[(sT, dt_T)])
            bank_free[pk] = (sV, d0_v)
            a_s = sop(lambda s, c=ch: s.activation(
                bp6[:, 0], bdelta[:, c], AF.Abs, bias=ZERO1, scale=ONE1),
                waits=[(sV, d0_v), p6_free[0]])
            e_s = sop(lambda s: s.activation(bp6[:, 1], bp6[:, 0], AF.Exp,
                                             bias=ZERO1, scale=NEG1F),
                      waits=[p6_free[1]])
            p6_free[0] = (sS, e_s)
            p1_v = vop(lambda v: v.tensor_scalar_add(bp6[:, 1], bp6[:, 1], 1.0),
                       waits=[(sS, e_s)])
            l_s = sop(lambda s: s.activation(bp6[:, 0], bp6[:, 1], AF.Ln,
                                             bias=ZERO1, scale=ONE1),
                      waits=[(sV, p1_v), p6_free[0]])
            p6_free[1] = (sS, l_s)
            delta_done[ch] = vop(lambda v, c=ch: v.scalar_tensor_tensor(
                bdelta[:, c].bitcast(F32R), bdelta[:, c], 0.0, bp6[:, 0],
                ALU.max, ALU.add), waits=[(sS, l_s)])
            p6_free[0] = (sV, delta_done[ch])

        du_done = [0] * KI
        for ch in range(KI):
            vop(lambda v, c=ch: v.tensor_scalar(
                by[:, c], bu[:, c], bdsk[:, c:c + 1], None, ALU.mult),
                waits=[(sS, u_done[ch])])
            du_done[ch] = vop(lambda v, c=ch: v.tensor_mul(
                bu[:, c].bitcast(F32R), bdelta[:, c], bu[:, c]),
                waits=[(sV, delta_done[ch])])
        all_du = sV.n

        import os as _os2
        if _os2.environ.get("RSER") == "scan":
            ser_now[0] = True
        slot_da = [(sV, 0), (sV, 0)]
        slot_dbu = [(sV, 0), (sV, 0)]
        slot_h = [(sGc, 0), (sGc, 0)]
        crep_free = (sV, 0)
        for n in range(NST):
            mm(pRB0[:, :], id(pRB0), beye[:, n:n + 1].broadcast_to([16, 128]),
               bBC[:, :], start=True, stop=True,
               extra_waits=[(sV, bc_done)], incs=[(sT, 1)])
            brep_T = sT.n
            mm(pRC[:, :], id(pRC), beye[:, n:n + 1].broadcast_to([16, 128]),
               bCsb[:, :], start=True, stop=True,
               extra_waits=[(sV, bc_done)], incs=[(sT, 1)])
            crep_T = sT.n
            crep_v = vop(lambda v: v.tensor_copy(bcrep[:, :], pRC[:, :]),
                         waits=[(sT, crep_T), crep_free])
            bank_free[id(pRC)] = (sV, crep_v)
            last_dbu = 0
            scan_of = {}

            def tail(k):
                q = k % 2
                vop(lambda v, qq=q, c=k, nn=n: v.tensor_copy(
                    bcar[:, c, nn:nn + 1], bhh[:, qq, TC - 1:TC]))
                emit("gpsimd", lambda g, qq=q: g.tensor_tensor(
                    bdbu[:, qq], bhh[:, qq], bcrep[:, :], ALU.mult),
                    waits=[(sV, scan_of[k])], incs=[(sGc, 1)])
                mult_g = sGc.n
                slot_h[q] = (sGc, mult_g)
                ya = vop(lambda v, c=k, qq=q: v.tensor_add(
                    by[:, c], by[:, c], bdbu[:, qq]),
                    waits=[(sGc, mult_g)])
                slot_dbu[q] = (sV, ya)

            for ch in range(KI):
                sl = ch % 2
                da_s = sop(lambda s, c=ch, q=sl, nn=n: s.activation(
                    bda[:, q], bdelta[:, c], AF.Exp, bias=ZERO1,
                    scale=bA[:, c, nn:nn + 1]),
                    waits=[(sV, delta_done[ch]), slot_da[sl]])
                vop(lambda v, c=ch, q=sl, p=pRB0: v.tensor_mul(
                    bdbu[:, q], bu[:, c], p[:, :]),
                    waits=[(sV, du_done[ch]), (sT, brep_T), slot_dbu[sl]])
                last_dbu = sV.n
                scan_v = vop(lambda v, q=sl, c=ch, nn=n: v.tensor_tensor_scan(
                    bhh[:, q], bda[:, q], bdbu[:, q], bcar[:, c, nn:nn + 1],
                    ALU.mult, ALU.add),
                    waits=[(sS, da_s), slot_h[sl]])
                slot_da[sl] = (sV, scan_v)
                scan_of[ch] = scan_v
                if ch >= 1:
                    tail(ch - 1)
            vop(lambda v: v.tensor_copy(bp6[:, 2, 0:256], bp6[:, 2, 256:512]))
            tail(KI - 1)
            bank_free[id(pRB0)] = (sV, last_dbu)
            crep_free = (sGc, sGc.n)
        y_scan_done = sV.n
        if _os2.environ.get("RSER") == "scan":
            ser_now[0] = False

        if dbg_heavy:
            emit("scalar", lambda s, tt=t0: s.dma_start(
                dbg_u[:, :, tt:tt + TC], bu[:, :, :]),
                waits=[(sV, all_du)], incs=[(sSt, 16)])
            emit("scalar", lambda s, tt=t0: s.dma_start(
                dbg_dl[:, :, tt:tt + TC], bdelta[:, :, :]),
                waits=[(sV, all_du)], incs=[(sSt, 16)])
            emit("scalar", lambda s, tt=t0: s.dma_start(
                dbg_y[:, :, tt:tt + TC], by[:, :, :]),
                waits=[(sV, y_scan_done)], incs=[(sSt, 16)])

        for cp in range(8, 16):
            slot = wr_ctr[0] % 2
            wr_ctr[0] += 1
            emit("sync", lambda sy, sl=slot, c=cp: sy.dma_start(
                bwr[:, sl, 0:KD], w_inB[c].transpose([1, 0, 2, 3])),
                waits=[wr_free[slot], (sG, CASTS_SMALL)], incs=[(sL, 16)])
            rdy = sL.n
            for c2 in range(2):
                ch = (cp - 8) * 2 + c2
                pp, pk = (pA, id(pA)) if ch % 2 == 0 else (pB, id(pB))
                for k in range(KD):
                    w0 = ([(sL, rdy)] if k == 0 else [])
                    mm(pp[:, :], pk, bwr[:, slot, k, c2, :], bxn1[:, k],
                       start=(k == 0), stop=(k == KD - 1),
                       extra_waits=w0,
                       incs=([(sT, 1)] if k == KD - 1 else []))
                z_T = sT.n
                if c2 == 1:
                    wr_free[slot] = (sT, z_T)
                zsl = ch % 2
                z_s = sop(lambda s, q=zsl, p=pp: s.activation(
                    bp6[:, q], p[:, :], AF.Silu, bias=ZERO1, scale=ONE1),
                    waits=[(sT, z_T), p6_free[zsl]])
                r = vop(lambda v, c=ch, q=zsl: v.tensor_mul(
                    byb[:, c], by[:, c], bp6[:, q]),
                    waits=[(sS, z_s), (sV, y_scan_done)])
                p6_free[zsl] = (sV, r)
                bank_free[pk] = (sV, r)
        yb_done = sV.n

        for dp in range(4):
            slot = wr_ctr[0] % 2
            wr_ctr[0] += 1
            emit("sync", lambda sy, sl=slot, d=dp: sy.dma_start(
                bwr[:, sl], w_outB[d].transpose([1, 0, 2, 3])),
                waits=[wr_free[slot], (sG, CASTS_SMALL)], incs=[(sL, 16)])
            rdy = sL.n
            for d2 in range(2):
                dch = dp * 2 + d2
                pp, pk = (pA, id(pA)) if dch % 2 == 0 else (pB, id(pB))
                for k in range(KI):
                    w0 = ([(sL, rdy), (sV, yb_done)] if k == 0 else [])
                    mm(pp[:, :], pk, bwr[:, slot, k, d2, :], byb[:, k],
                       start=(k == 0), stop=(k == KI - 1),
                       extra_waits=w0,
                       incs=([(sT, 1)] if k == KI - 1 else []))
                op_T = sT.n
                if d2 == 1:
                    wr_free[slot] = (sT, op_T)
                r = vop(lambda v, c=dch, p=pp: v.tensor_copy(
                    bmo[:, c].bitcast(F32R), p[:, :]), waits=[(sT, op_T)])
                bank_free[pk] = (sV, r)
        mo_all = sV.n

        rows2 = _ln(KD, lambda c: bmo[:, c], lambda c: bmo[:, c].bitcast(F32R),
                    [(sV, mo_all)])

        def fin2(ch, sl):
            r = vop(lambda v, c=ch, q=sl: v.tensor_add(
                bx[:, c], bx[:, c].bitcast(F32), bp6[:, q]))
            p6_free[sl] = (sV, r)
        x2_done = ln_apply(KD, rows2, lambda c: bmo[:, c], fin2)
        emit("scalar", lambda s, tt=t0: s.dma_start(
            x2s[:, :, tt:tt + TC], bx[:, :, :].bitcast(F32)),
            waits=[(sV, x2_done)], incs=[(sSt, 16)])
        x2_store = sSt.n
        if debug:
            emit("scalar", lambda s, tt=t0: s.dma_start(
                dbg_x2[:, :, tt:tt + TC], bx[:, :, :].bitcast(F32)),
                waits=[(sV, x2_done)], incs=[(sSt, 16)])

        rows3 = _ln(KD, lambda c: bx[:, c].bitcast(F32), lambda c: bx[:, c],
                    [(sV, x2_done)])

        def fin3(ch, sl):
            xv = sV.n
            emit("scalar", lambda s, c=ch, q=sl, tt=t0: s.dma_start(
                xn3s[:, c, tt:tt + TC], bp6[:, q]),
                waits=[(sV, xv)], incs=[(sSt, 16)])
            stv = sSt.n
            vop(lambda v, q=sl, c=ch: v.reduce_sum(
                bred[:, c:c + 1], bp6[:, q], axis=AX.X))
            p6_free[sl] = (sSt, stv)
            xn3_stores.append(stv)
        xn3_stores = []
        ln3_done = ln_apply(KD, rows3, lambda c: bx[:, c].bitcast(F32), fin3)
        vop(lambda v: v.tensor_copy(bp6[:, 2, 0:256], bp6[:, 2, 256:512]))
        vop(lambda v: v.tensor_add(bpool[:, :], bpool[:, :], bred[:, :]))
        for sl in (0, 1):
            p6_free[sl] = (sSt, xn3_stores[-1])

        if i + 1 < NT:
            emit("gpsimd", lambda g, tt=(i + 1) * TC: g.dma_start(
                bx[:, :, :], xTr[:, :, tt:tt + TC]),
                waits=[(sV, ln3_done), (sSt, x2_store)], incs=[(sG, 16)])
            XLOAD.append(sG.n)

    s1.close()
    s2 = ExitStack()
    e2 = s2.enter_context
    bw1 = e2(nc.sbuf_tensor("bw1", [128, KD, RH], F32R))
    bw2 = e2(nc.sbuf_tensor("bw2", [128, 2, 8], F32R))
    brr = e2(nc.sbuf_tensor("brr", [1, 64], F32))
    bwrow = e2(nc.sbuf_tensor("bwrow", [1, 8], F32R))
    bmsk = e2(nc.sbuf_tensor("bmsk", [1, 8], mybir.dt.uint8))
    bhrt = e2(nc.sbuf_tensor("bhrt", [128, 2, 128], F32R))
    bhb = e2(nc.sbuf_tensor("bhb", [128, 2], F32))
    bxn3b = e2(nc.sbuf_tensor("bxn3b", [128, KD, TC], BF16))
    bxn3f = e2(nc.sbuf_tensor("bxn3f", [128, KD, TC], F32))
    bH = e2(nc.sbuf_tensor("bH", [128, KF, TC], BF16))
    bzac = e2(nc.sbuf_tensor("bzac", [128, KD, TC], F32))
    bwa = e2(nc.sbuf_tensor("bwa", [128, 4, KD, 2, 128], BF16))
    bwb = e2(nc.sbuf_tensor("bwb", [128, 3, KF, 2, 128], BF16))

    P1BAR = [(sV, sV.n), (sT, sT.n), (sS, sS.n), (sGc, sGc.n),
             (sL, sL.n), (sG, sG.n), (sSt, sSt.n)]

    import os as _os
    if _os.environ.get("RSER"):
        ser_now[0] = True
    emit("gpsimd", lambda g: g.dma_start(
        bw1[:, :, :], w1h.rearrange("(c p) e -> p c e", p=128)),
        waits=P1BAR, incs=[(sG, 16)])
    emit("gpsimd", lambda g: g.dma_start(
        bw2[:, :, :], w2h.rearrange("(c p) e -> p c e", p=128)),
        waits=P1BAR, incs=[(sG, 16)])
    emit("gpsimd", lambda g: g.dma_start(
        bhrt[:, :, :].rearrange("p a b -> p (a b)"), z2x128),
        waits=P1BAR, incs=[(sG, 16)])
    W12 = sG.n
    pool_v = vop(lambda v: v.tensor_scalar_mul(
        bpoolr[:, :], bpool[:, :], 1.0 / L), waits=P1BAR)
    for j in range(2):
        for k in range(KD):
            inc = [(sT, 1)] if (j == 1 and k == KD - 1) else []
            mm(pROW[:, j * 8:(j + 1) * 8], id(pROW),
               bw1[:, k, j * 128:(j + 1) * 128],
               bpoolr[:, k:k + 1].broadcast_to([128, 8]),
               start=(j == 0 and k == 0), stop=(k == KD - 1),
               extra_waits=[(sV, pool_v), (sG, W12)], incs=inc)
    rt_T = sT.n

    def vpad():
        return vop(lambda v: v.tensor_copy(bp6[:, 2, 0:256],
                                           bp6[:, 2, 256:512]))

    for j in range(2):
        vop(lambda v, jj=j: v.tensor_scalar(
            bhb[:, jj:jj + 1], pROW[:, jj * 8:jj * 8 + 1],
            bb1[:, jj:jj + 1], None, ALU.add), waits=[(sT, rt_T)])
    hb_v = vpad()
    h_s = sop(lambda s: s.activation(
        bhrt[:, :, 0], bhb[:, :], AF.Tanh, bias=ZERO1, scale=ONE1),
        waits=[(sV, hb_v), (sG, W12)])
    bank_free[id(pROW)] = (sS, h_s)
    for j in range(2):
        inc = [(sT, 1)] if j == 1 else []
        mm(pXDB[:, 0:8], id(pXDB), bhrt[:, j, :], bw2[:, j, :],
           start=(j == 0), stop=(j == 1), extra_waits=[(sS, h_s)], incs=inc)
    lg_T = sT.n
    vop(lambda v: v.tensor_add(brr[:, 0:8], pXDB[0:1, 0:8], bb2[:, :]),
        waits=[(sT, lg_T)])
    bank_free[id(pXDB)] = (sV, sV.n)
    vpad()
    vop(lambda v: v.reduce_max(brr[:, 8:9], brr[:, 0:8], axis=AX.X))
    vpad()
    vop(lambda v: v.tensor_sub(brr[:, 0:8], brr[:, 0:8],
                               brr[:, 8:9].broadcast_to([1, 8])))
    sm_v = vpad()
    sm_s = sop(lambda s: s.activation(brr[:, 16:24], brr[:, 0:8], AF.Exp,
                                      bias=ZEROR, scale=ONER),
               waits=[(sV, sm_v)])
    vop(lambda v: v.reduce_sum(brr[:, 8:9], brr[:, 16:24], axis=AX.X),
        waits=[(sS, sm_s)])
    vpad()
    vop(lambda v: v.reciprocal(brr[:, 9:10], brr[:, 8:9]))
    vpad()
    vop(lambda v: v.tensor_mul(brr[:, 16:24], brr[:, 16:24],
                               brr[:, 9:10].broadcast_to([1, 8])))
    vpad()
    vop(lambda v: v.tensor_copy(brr[:, 24:32], brr[:, 16:24]))
    vpad()
    for rnd in range(3):
        vop(lambda v: v.reduce_max(brr[:, 8:9], brr[:, 24:32], axis=AX.X))
        vpad()
        if rnd < 2:
            vop(lambda v: v.tensor_tensor(
                bmsk[:, :], brr[:, 24:32],
                brr[:, 8:9].broadcast_to([1, 8]), ALU.is_ge))
            vpad()
            vop(lambda v: v.select(brr[:, 24:32], bmsk[:, :],
                                   bninf[:, :], brr[:, 24:32]))
            vpad()
    vop(lambda v: v.tensor_tensor(brr[:, 32:40], brr[:, 16:24],
                                  brr[:, 8:9].broadcast_to([1, 8]), ALU.is_ge))
    vpad()
    vop(lambda v: v.tensor_mul(brr[:, 24:32], brr[:, 16:24], brr[:, 32:40]))
    vpad()
    vop(lambda v: v.reduce_sum(brr[:, 8:9], brr[:, 24:32], axis=AX.X))
    vpad()
    vop(lambda v: v.reciprocal(brr[:, 9:10], brr[:, 8:9]))
    vpad()
    vop(lambda v: v.tensor_mul(
        bwrow[:, :], brr[:, 24:32],
        brr[:, 9:10].broadcast_to([1, 8])))
    wrow_v = vpad()
    mm(pROW[:, 0:8], id(pROW), bonesrow[:, :], bwrow[:, :],
       start=True, stop=True, extra_waits=[(sV, wrow_v)], incs=[(sT, 1)])
    wz_T = sT.n
    vop(lambda v: v.tensor_copy(bwz[:, :], pROW[:, 0:8]),
        waits=[(sT, wz_T)])
    WZ = vpad()
    bank_free[id(pROW)] = (sV, WZ)
    if debug:
        emit("scalar", lambda s: s.dma_start(dbg_rr, brr[:, :]),
             waits=[(sV, WZ)], incs=[(sSt, 16)])
        emit("scalar", lambda s: s.dma_start(dbg_pool, bpoolr[:, :].bitcast(F32)),
             waits=[(sV, WZ)], incs=[(sSt, 16)])
        emit("scalar", lambda s: s.dma_start(
            dbg_h, bhrt[:, :, 0].bitcast(F32), allow_slow_non_contiguous=True),
             waits=[(sV, WZ)], incs=[(sSt, 16)])
        emit("scalar", lambda s: s.dma_start(dbg_w, bwz[:, :]),
             waits=[(sV, WZ)], incs=[(sSt, 16)])

    if _os.environ.get("RSER") == "router":
        ser_now[0] = False
    wa_free = [(sT, 0), (sT, 0), (sT, 0), (sT, 0)]
    wb_free = [(sT, 0), (sT, 0), (sT, 0)]
    h_free_T = 0
    xn3_used = 0
    x2_used = 0
    wa_ctr = [0]
    wb_ctr = [0]
    for st in range(NT if not SKIP2 else 1):
        t0 = st * TC
        bar = P1BAR if st == 0 else []
        emit("sync", lambda sy, tt=t0: sy.dma_start(
            bxn3f[:, :, :], xn3s[:, :, tt:tt + TC]),
            waits=list(bar) + [(sV, xn3_used)], incs=[(sL, 16)])
        xn3f_ld = sL.n
        emit("gpsimd", lambda g, tt=t0: g.dma_start(
            bx[:, :, :], x2s[:, :, tt:tt + TC]),
            waits=list(bar) + [(sV, x2_used)], incs=[(sG, 16)])
        x2_ld = sG.n
        for ch in range(KD):
            vop(lambda v, c=ch: v.tensor_copy(bxn3b[:, c], bxn3f[:, c]),
                waits=[(sL, xn3f_ld)])
        xn3b_v = sV.n

        for ez in range(NZ):
            h_done = [0] * KF
            wa_rdy = 0
            for fc in range(KF):
                if fc % 2 == 0:
                    slot = wa_ctr[0] % 4
                    wa_ctr[0] += 1
                    emit("sync", lambda sy, z=ez, f=fc, sl=slot: sy.dma_start(
                        bwa[:, sl], waB[z, f // 2].transpose([1, 0, 2, 3])),
                        waits=list(bar) + [wa_free[slot], (sG, CASTS_DONE)],
                        incs=[(sL, 16)])
                    wa_rdy = sL.n
                    cur_wa = slot
                pp, pk = [(pA, id(pA)), (pB, id(pB)),
                          (pROW, id(pROW))][fc % 3]
                for k in range(KD):
                    w0 = ([(sL, wa_rdy), (sV, xn3b_v)] if k == 0 else [])
                    mm(pp[:, :], pk, bwa[:, cur_wa, k, fc % 2, :], bxn3b[:, k],
                       start=(k == 0), stop=(k == KD - 1),
                       extra_waits=w0,
                       incs=([(sT, 1)] if k == KD - 1 else []))
                hT = sT.n
                if fc % 2 == 1:
                    wa_free[cur_wa] = (sT, hT)
                h_done[fc] = sop(lambda s, f=fc, p=pp: s.activation(
                    bH[:, f], p[:, :], AF.Gelu_apprx_tanh,
                    bias=ZERO1, scale=ONE1), waits=[(sT, hT)])
                bank_free[pk] = (sS, h_done[fc])
            for dp in range(4):
                slot = wb_ctr[0] % 3
                wb_ctr[0] += 1
                emit("sync", lambda sy, z=ez, d=dp, sl=slot: sy.dma_start(
                    bwb[:, sl], wbB[z, d].transpose([1, 0, 2, 3])),
                    waits=list(bar) + [wb_free[slot], (sG, CASTS_DONE)],
                    incs=[(sL, 16)])
                wb_rdy = sL.n
                for d2 in range(2):
                    dch = dp * 2 + d2
                    pp, pk = [(pRB0, id(pRB0)), (pRC, id(pRC)),
                              (pXDB, id(pXDB))][dch % 3]
                    for fc in range(KF):
                        w0 = ([(sL, wb_rdy), (sS, h_done[KF - 1])]
                              if fc == 0 else [])
                        mm(pp[:, :], pk, bwb[:, slot, fc, d2, :], bH[:, fc],
                           start=(fc == 0), stop=(fc == KF - 1),
                           extra_waits=w0,
                           incs=([(sT, 1)] if fc == KF - 1 else []))
                    zT = sT.n
                    if d2 == 1:
                        wb_free[slot] = (sT, zT)
                    if ez == 0:
                        r = vop(lambda v, c=dch, p=pp: v.scalar_tensor_tensor(
                            bzac[:, c].bitcast(F32R), p[:, :], bwz[:, 0:1],
                            bxn3f[:, c], ALU.mult, ALU.add),
                            waits=[(sT, zT), (sV, WZ), (sL, xn3f_ld)])
                    else:
                        r = vop(lambda v, c=dch, p=pp, z=ez:
                                v.scalar_tensor_tensor(
                                    bzac[:, c].bitcast(F32R), p[:, :],
                                    bwz[:, z:z + 1], bzac[:, c],
                                    ALU.mult, ALU.add), waits=[(sT, zT)])
                    bank_free[pk] = (sV, r)
            h_free_T = sT.n
        xn3_used = sV.n

        f_done = sV.n

        rows4 = _ln(KD, lambda c: bzac[:, c],
                    lambda c: bzac[:, c].bitcast(F32R), [(sV, f_done)])

        def fin4(ch, sl):
            r = vop(lambda v, c=ch, q=sl: v.tensor_add(
                bx[:, c], bx[:, c].bitcast(F32), bp6[:, q]),
                waits=[(sG, x2_ld)])
            p6_free[sl] = (sV, r)
        out_v = ln_apply(KD, rows4, lambda c: bzac[:, c], fin4)
        x2_used = out_v
        emit("scalar", lambda s, tt=t0: s.dma_start(
            out[:, :, tt:tt + TC], bx[:, :, :].bitcast(F32)),
            waits=[(sV, out_v)], incs=[(sSt, 16)])

    FIN = sSt.n

    def fin_wait(h):
        h.wait_ge(sSt.h, FIN)
    OPS["scalar"].append(fin_wait)

    s2.close()
    build.LBL = LBL
    with nc.allow_low_precision(reason="bf16/f32r matmul operands; fp32 accumulation everywhere"), nc.Block() as blk:
        @blk.sync
        def _(h):
            for f in OPS["sync"]:
                f(h)

        @blk.gpsimd
        def _(h):
            for f in OPS["gpsimd"]:
                f(h)

        @blk.tensor
        def _(h):
            for f in OPS["tensor"]:
                f(h)

        @blk.vector
        def _(h):
            for f in OPS["vector"]:
                f(h)

        @blk.scalar
        def _(h):
            for f in OPS["scalar"]:
                f(h)

    es.close()
    return nc


def _prep_weights(inp):
    f32 = np.float32
    w = {}
    ipw = np.asarray(inp["L0_attn_in_proj_w"], f32)
    w_inT = np.ascontiguousarray(ipw.T)
    w["w_inT"] = np.ascontiguousarray(
        w_inT.reshape(KD, 128, 16, 2, 128).transpose(2, 0, 1, 3, 4))
    opw = np.asarray(inp["L0_attn_out_proj_w"], f32)
    w_outT = np.ascontiguousarray(opw.T)
    w["w_outT"] = np.ascontiguousarray(
        w_outT.reshape(KI, 128, 4, 2, 128).transpose(2, 0, 1, 3, 4))
    xp = np.asarray(inp["L0_attn_x_proj_w"], f32).T
    xpp = np.zeros((DI, 128), f32)
    xpp[:, 0:64] = xp[:, 0:64]
    xpp[:, 64:80] = xp[:, 64:80]
    xpp[:, 96:112] = xp[:, 80:96]
    w["w_xpT"] = np.ascontiguousarray(
        xpp.reshape(KI, 128, 128).transpose(1, 0, 2))
    w["w_dtT"] = np.ascontiguousarray(np.asarray(inp["L0_attn_dt_proj_w"], f32).T)
    w["convw"] = np.ascontiguousarray(
        np.asarray(inp["L0_attn_conv_w"], f32)[:, 0, :]
        .reshape(KI, 128, 4).transpose(1, 0, 2))
    w["convb"] = np.ascontiguousarray(
        np.asarray(inp["L0_attn_conv_b"], f32).reshape(KI, 128).T)
    w["dtb"] = np.ascontiguousarray(
        np.asarray(inp["L0_attn_dt_bias"], f32).reshape(KI, 128).T)
    w["alog"] = np.ascontiguousarray(
        np.asarray(inp["L0_attn_A_log"], f32).reshape(KI, 128, NST)
        .transpose(1, 0, 2))
    w["dsk"] = np.ascontiguousarray(
        np.asarray(inp["L0_attn_D_skip"], f32).reshape(KI, 128).T)
    w["w1h"] = np.ascontiguousarray(np.asarray(inp["L0_ffn_W1"], f32))
    w["b1c"] = np.ascontiguousarray(
        np.asarray(inp["L0_ffn_b1"], f32).reshape(2, 128).T)
    w["w2h"] = np.ascontiguousarray(np.asarray(inp["L0_ffn_W2"], f32))
    w["b2r"] = np.ascontiguousarray(np.asarray(inp["L0_ffn_b2"], f32).reshape(1, 8))
    Wa = np.asarray(inp["L0_ffn_Wa"], f32)
    w["wah"] = np.ascontiguousarray(
        Wa.reshape(NZ, KD, 128, 16, 2, 128).transpose(0, 3, 1, 2, 4, 5))
    Wb = np.asarray(inp["L0_ffn_Wb"], f32)
    w["wbh"] = np.ascontiguousarray(
        Wb.reshape(NZ, KF, 128, 4, 2, 128).transpose(0, 3, 1, 2, 4, 5))
    cstc = np.zeros((128, 8), f32)
    cstc[:, 0] = 1.0
    cstc[:, 2] = -1.0
    cstc[:, 5] = 1.0
    w["cst"] = cstc
    cms = np.zeros((128, 256), f32)
    cms[:, 0] = 1.0
    cms[:, 128 + 32] = 1.0
    w["cstMS"] = cms
    w["z2x128"] = np.zeros((128, 256), f32)
    w["eyeh"] = np.eye(16, dtype=f32)
    w["onesrow"] = np.ones((1, 128), f32)
    w["ninf"] = np.full((1, 8), -1e30, f32)
    return w


def kernel(**inputs):
    x = np.asarray(inputs["L0_attn_hidden_states"], np.float32)
    B, L, _ = x.shape
    wts = _prep_weights(inputs)
    nc = build(L)
    in_maps = []
    for b in range(B):
        m = dict(wts)
        m["xT"] = np.ascontiguousarray(x[b].T)
        in_maps.append(m)
    res = run_bass_kernel_spmd(nc, in_maps, core_ids=list(range(B)))
    outs = []
    for r in res.results:
        o = np.asarray(r["out"]).reshape(128, KD, L)
        od = np.ascontiguousarray(o.transpose(1, 0, 2).reshape(D, L).T)
        outs.append(od)
    return np.stack(outs).astype(np.float32)


# version 04
# speedup vs baseline: 1.4166x; 1.0377x over previous
import numpy as np
from contextlib import ExitStack

import concourse.bass as bass
import concourse.mybir as mybir
from concourse.bass_utils import run_bass_kernel_spmd

F32 = mybir.dt.float32
F32R = mybir.dt.float32r
BF16 = mybir.dt.bfloat16
AF = mybir.ActivationFunctionType
ALU = mybir.AluOpType
AX = mybir.AxisListType

D = 1024
DI = 2048
NST = 16
RK = 64
KD = D // 128
KI = DI // 128
TC = 512
NZ = 8
DFF = 4096
KF = DFF // 128
RH = 256
EPS = 1e-5


class Sem:
    def __init__(self, es, nc, name):
        self.h = es.enter_context(nc.semaphore(name))
        self.n = 0


def build(L, debug=False, serialize=False):
    import os as _osd
    dbg_heavy = debug and not _osd.environ.get("NOHEAVY")
    nc = bass.Bass("TRN2", target_bir_lowering=False)
    NT = L // TC

    xT = nc.dram_tensor("xT", [D, L], F32, kind="ExternalInput").ap()
    w_inT = nc.dram_tensor("w_inT", [16, KD, 128, 2, 128], F32, kind="ExternalInput").ap()
    w_outT = nc.dram_tensor("w_outT", [4, KI, 128, 2, 128], F32, kind="ExternalInput").ap()
    w_xpT = nc.dram_tensor("w_xpT", [128, KI, 128], F32, kind="ExternalInput").ap()
    w_dtT = nc.dram_tensor("w_dtT", [RK, DI], F32, kind="ExternalInput").ap()
    convw = nc.dram_tensor("convw", [128, KI, 4], F32, kind="ExternalInput").ap()
    convb = nc.dram_tensor("convb", [128, KI], F32, kind="ExternalInput").ap()
    dtb = nc.dram_tensor("dtb", [128, KI], F32, kind="ExternalInput").ap()
    alog = nc.dram_tensor("alog", [128, KI, NST], F32, kind="ExternalInput").ap()
    dsk = nc.dram_tensor("dsk", [128, KI], F32, kind="ExternalInput").ap()
    w1h = nc.dram_tensor("w1h", [D, RH], F32, kind="ExternalInput").ap()
    b1c = nc.dram_tensor("b1c", [128, 2], F32, kind="ExternalInput").ap()
    w2h = nc.dram_tensor("w2h", [RH, 8], F32, kind="ExternalInput").ap()
    b2r = nc.dram_tensor("b2r", [1, 8], F32, kind="ExternalInput").ap()
    wah = nc.dram_tensor("wah", [NZ, 16, KD, 128, 2, 128], F32, kind="ExternalInput").ap()
    wbh = nc.dram_tensor("wbh", [NZ, 4, KF, 128, 2, 128], F32, kind="ExternalInput").ap()
    cst = nc.dram_tensor("cst", [128, 8], F32, kind="ExternalInput").ap()
    cstMS = nc.dram_tensor("cstMS", [128, 256], F32, kind="ExternalInput").ap()
    z2x128 = nc.dram_tensor("z2x128", [128, 256], F32, kind="ExternalInput").ap()
    eyeh = nc.dram_tensor("eyeh", [16, NST], F32, kind="ExternalInput").ap()
    ninf = nc.dram_tensor("ninf", [1, 8], F32, kind="ExternalInput").ap()
    onesrow_h = nc.dram_tensor("onesrow", [1, 128], F32, kind="ExternalInput").ap()

    w_inB = nc.dram_tensor("w_inB", [16, KD, 128, 2, 128], BF16).ap()
    w_outB = nc.dram_tensor("w_outB", [4, KI, 128, 2, 128], BF16).ap()
    waB = nc.dram_tensor("waB", [NZ, 16, KD, 128, 2, 128], BF16).ap()
    wbB = nc.dram_tensor("wbB", [NZ, 4, KF, 128, 2, 128], BF16).ap()
    if debug:
        x2s = nc.dram_tensor("x2s", [128, KD, L], F32, kind="ExternalOutput").ap()
        xn3s = nc.dram_tensor("xn3s", [128, KD, L], F32, kind="ExternalOutput").ap()
    else:
        x2s = nc.dram_tensor("x2s", [128, KD, L], F32).ap()
        xn3s = nc.dram_tensor("xn3s", [128, KD, L], F32).ap()
    out = nc.dram_tensor("out", [128, KD, L], F32, kind="ExternalOutput").ap()
    if debug:
        dbg_u = nc.dram_tensor("dbg_u", [128, KI, L], F32, kind="ExternalOutput").ap()
        dbg_dl = nc.dram_tensor("dbg_dl", [128, KI, L], F32, kind="ExternalOutput").ap()
        dbg_y = nc.dram_tensor("dbg_y", [128, KI, L], F32, kind="ExternalOutput").ap()
        dbg_w = nc.dram_tensor("dbg_w", [128, 8], F32, kind="ExternalOutput").ap()
        dbg_x2 = nc.dram_tensor("dbg_x2", [128, KD, L], F32, kind="ExternalOutput").ap()
        dbg_xn1 = nc.dram_tensor("dbg_xn1", [128, KD, L], BF16, kind="ExternalOutput").ap()
        dbg_rr = nc.dram_tensor("dbg_rr", [1, 64], F32, kind="ExternalOutput").ap()
        dbg_pool = nc.dram_tensor("dbg_pool", [128, KD], F32, kind="ExternalOutput").ap()
        dbg_h = nc.dram_tensor("dbg_h", [128, 2], F32, kind="ExternalOutput").ap()
        dbg_xh = nc.dram_tensor("dbg_xh", [128, KI, L], F32, kind="ExternalOutput").ap()

    es = ExitStack()
    e = es.enter_context
    s1 = ExitStack()
    e1 = s1.enter_context

    bx = e(nc.sbuf_tensor("bx", [128, KD, TC], F32R))
    bp6 = e(nc.sbuf_tensor("bp6", [128, 3, TC], F32))
    brow = e(nc.sbuf_tensor("brow", [1, 3, TC], F32))
    browR = e(nc.sbuf_tensor("browR", [1, 2, TC], F32R))
    bsq = e(nc.sbuf_tensor("bsq", [128, 2, TC], F32R))
    bred = e(nc.sbuf_tensor("bred", [128, KD], F32))
    bpool = e(nc.sbuf_tensor("bpool", [128, KD], F32))
    bcstF = e(nc.sbuf_tensor("bcstF", [128, 8], F32))
    bcstR = e(nc.sbuf_tensor("bcstR", [128, 8], F32R))
    bcstMS = e(nc.sbuf_tensor("bcstMS", [128, 256], F32R))
    bonesrow = e(nc.sbuf_tensor("bonesrow", [1, 128], F32R))
    bninf = e(nc.sbuf_tensor("bninf", [1, 8], F32))
    bb2 = e(nc.sbuf_tensor("bb2", [1, 8], F32))
    bb1 = e(nc.sbuf_tensor("bb1", [128, 2], F32))
    bwz = e(nc.sbuf_tensor("bwz", [128, 8], F32))
    bpoolr = e(nc.sbuf_tensor("bpoolr", [128, KD], F32R))

    bxn1 = e1(nc.sbuf_tensor("bxn1", [128, KD, TC], BF16))
    bu = e1(nc.sbuf_tensor("bu", [128, KI, TC], F32))
    bdelta = e1(nc.sbuf_tensor("bdelta", [128, KI, TC], F32))
    by = e1(nc.sbuf_tensor("by", [128, KI, TC], F32))
    byb = e1(nc.sbuf_tensor("byb", [128, KI, TC], BF16))
    bmo = bdelta[:, 0:KD, :]
    bconv = e1(nc.sbuf_tensor("bconv", [128, 1, TC + 4], F32))
    bhalo = e1(nc.sbuf_tensor("bhalo", [128, KI, 3], F32))
    bda = e1(nc.sbuf_tensor("bda", [128, 2, TC], F32))
    bdbu = e1(nc.sbuf_tensor("bdbu", [128, 2, TC], F32))
    bhh = e1(nc.sbuf_tensor("bhh", [128, 2, TC], F32))
    bcrep = e1(nc.sbuf_tensor("bcrep", [128, TC], F32))
    bcar = e1(nc.sbuf_tensor("bcar", [128, KI, NST], F32))
    bdtr = e1(nc.sbuf_tensor("bdtr", [64, TC], BF16))
    bBC = e1(nc.sbuf_tensor("bBC", [16, TC], BF16))
    bCsb = e1(nc.sbuf_tensor("bCsb", [16, TC], BF16))
    beye = e1(nc.sbuf_tensor("beye", [16, NST], BF16))
    bwr = e1(nc.sbuf_tensor("bwr", [128, 2, KI, 2, 128], BF16))
    bxpj = e1(nc.sbuf_tensor("bxpj", [128, KI, 128], F32R))
    bdtp = e1(nc.sbuf_tensor("bdtp", [64, DI], BF16))
    bA = e1(nc.sbuf_tensor("bA", [128, KI, NST], F32))
    bdtb2 = e1(nc.sbuf_tensor("bdtb2", [128, KI], F32))
    bcw = e1(nc.sbuf_tensor("bcw", [128, KI, 4], F32))
    bcb = e1(nc.sbuf_tensor("bcb", [128, KI], F32))
    bdsk = e1(nc.sbuf_tensor("bdsk", [128, KI], F32))

    pA = e(nc.psum_tensor("pA", [128, TC], F32))
    pB = e(nc.psum_tensor("pB", [128, TC], F32))
    pXDB = e(nc.psum_tensor("pXDB", [128, TC], F32))
    pROW = e(nc.psum_tensor("pROW", [128, TC], F32))
    pRB0 = e(nc.psum_tensor("pRB0", [128, TC], F32))
    pRC = e(nc.psum_tensor("pRC", [128, TC], F32))

    sL = Sem(es, nc, "sL")
    sG = Sem(es, nc, "sG")
    sT = Sem(es, nc, "sT")
    sV = Sem(es, nc, "sV")
    sS = Sem(es, nc, "sS")
    sGc = Sem(es, nc, "sGc")
    sSt = Sem(es, nc, "sSt")

    OPS = {k: [] for k in ("sync", "gpsimd", "vector", "scalar", "tensor")}
    LBL = {k: [] for k in ("sync", "gpsimd", "vector", "scalar", "tensor")}
    import traceback

    ser_now = [False]
    import os as _os0
    ser_eng = _os0.environ.get("RSENG", "")

    def emit(eng, fn, waits=(), incs=()):
        LBL[eng].append(traceback.extract_stack(limit=3)[0].lineno)
        if serialize or (ser_now[0] and (not ser_eng or ser_eng == eng)):
            waits = list(waits) + [(s_, s_.n) for s_ in
                                   (sL, sG, sT, sV, sS, sGc, sSt)]
        wl = [(s.h, int(v)) for (s, v) in waits if v > 0]
        pl = [(s.h, k) for (s, k) in incs]
        for s, k in incs:
            s.n += k
        def run(h, wl=wl, pl=pl, fn=fn):
            for hh, v in wl:
                h.wait_ge(hh, v)
            ins = fn(h)
            for hh, k in pl:
                ins = ins.then_inc(hh, k)
        OPS[eng].append(run)

    CM = bcstMS[:, 0:128]
    CS = bcstMS[:, 128:256]
    NEG1F = bcstF[:, 2:3]
    ONE1 = bcstF[:, 0:1]
    ZERO1 = bcstF[:, 1:2]
    ONER = bcstF[0:1, 0:1]
    ZEROR = bcstF[0:1, 1:2]

    def g_dma(dst, src, incs=None):
        emit("gpsimd", lambda g, d=dst, s=src: g.dma_start(d, s),
             incs=incs if incs is not None else [(sG, 16)])

    g_dma(bcstR[:, :], cst)
    g_dma(bcstMS[:, :], cstMS)
    g_dma(bonesrow[:, :], onesrow_h)
    g_dma(beye[:, :], eyeh)
    g_dma(bxpj[:, :, :], w_xpT)
    g_dma(bdtp[:, :], w_dtT)
    g_dma(w_inB, w_inT)
    g_dma(w_outB, w_outT)
    CASTS_SMALL = sG.n
    for z in range(NZ):
        g_dma(waB[z], wah[z])
        g_dma(wbB[z], wbh[z])
    CASTS_DONE = sG.n

    xTr = xT.rearrange("(c p) t -> p c t", p=128)
    g_dma(bx[:, :, :], xTr[:, :, 0:TC])
    XLOAD = [sG.n]

    emit("gpsimd", lambda g: g.memzero(bcar[:, :, :]))
    emit("gpsimd", lambda g: g.memzero(bhalo[:, :, :]))
    emit("gpsimd", lambda g: g.memzero(bpool[:, :]), incs=[(sGc, 1)])
    GZERO = sGc.n

    def s_dma(dst, src):
        emit("sync", lambda sy, d=dst, s=src: sy.dma_start(d, s),
             incs=[(sL, 16)])

    s_dma(bcstF[:, :], cst)
    s_dma(bcw[:, :, :], convw)
    s_dma(bcb[:, :], convb)
    s_dma(bdtb2[:, :], dtb)
    s_dma(bp6[:, 0, 0:KI * NST], alog.rearrange("p a b -> p (a b)"))
    s_dma(bdsk[:, :], dsk)
    s_dma(bb1[:, :], b1c)
    s_dma(bninf[:, :], ninf)
    s_dma(bb2[:, :], b2r)
    RESIDENT = sL.n

    emit("vector", lambda v: v.tensor_scalar_mul(bdtb2[:, :], bdtb2[:, :], 2.0),
         waits=[(sL, RESIDENT)], incs=[(sV, 1)])
    emit("scalar", lambda s: s.activation(
        bA[:, :, :].rearrange("p a b -> p (a b)"), bp6[:, 0, 0:KI * NST],
        AF.Exp, bias=ZERO1, scale=ONE1), waits=[(sL, RESIDENT)], incs=[(sS, 1)])
    emit("vector", lambda v: v.tensor_scalar_mul(bA[:, :, :], bA[:, :, :], -1.0),
         waits=[(sS, sS.n)], incs=[(sV, 1)])
    PREP_V = sV.n
    p6_free_init_s = sS.n

    wr_free = [(sV, 0), (sV, 0)]
    wr_ctr = [0]

    bank_free = {id(p): (sV, 0) for p in (pA, pB, pXDB, pROW, pRB0, pRC)}
    p6_free = {i: (sV, 0) for i in range(3)}
    p6_free[0] = (sS, p6_free_init_s)

    def mm(p_ap, p_key, st_ap, mv_ap, start, stop, extra_waits=(), incs=()):
        w = list(extra_waits)
        if start:
            w.append(bank_free[p_key])
        emit("tensor", lambda t, o=p_ap, a=st_ap, b=mv_ap, s1=start, s2=stop:
             t.matmul(o, a, b, start=s1, stop=s2), waits=w, incs=incs)

    def vop(fn, waits=()):
        emit("vector", fn, waits=waits, incs=[(sV, 1)])
        return sV.n

    def sop(fn, waits=()):
        emit("scalar", fn, waits=waits, incs=[(sS, 1)])
        return sS.n

    sq_free = {0: (sT, 0), 1: (sT, 0)}

    def _ln(nch, src_f32, src_mov, ready_waits):
        for ch in range(nch):
            mm(pROW[:, :], id(pROW), CM, src_mov(ch),
               start=(ch == 0), stop=False, extra_waits=list(ready_waits))
        base_T = sT.n
        for ch in range(nch):
            sl = ch % 2
            w = list(ready_waits)
            if ch >= 2:
                w.append((sT, base_T + ch - 1))
            sqd = sop(lambda s, c=ch, q=sl: s.activation(
                bsq[:, q], src_f32(c), AF.Square, bias=ZERO1, scale=ONE1),
                waits=w)
            mm(pROW[:, :], id(pROW), CS, bsq[:, sl],
               start=False, stop=(ch == nch - 1),
               extra_waits=[(sS, sqd)], incs=[(sT, 1)])
        stats_T = sT.n
        inv = 1.0 / (nch * 128)
        vop(lambda v: v.tensor_scalar_mul(brow[:, 0], pROW[0:1, :], inv),
            waits=[(sT, stats_T)])
        vop(lambda v: v.tensor_scalar_mul(brow[:, 1], pROW[32:33, :], inv))
        vop(lambda v: v.tensor_mul(brow[:, 2], brow[:, 0], brow[:, 0]))
        vop(lambda v: v.tensor_sub(brow[:, 1], brow[:, 1], brow[:, 2]))
        rows_v = vop(lambda v: v.tensor_scalar_add(brow[:, 1], brow[:, 1], EPS))
        sqrt_s = sop(lambda s: s.activation(brow[:, 2], brow[:, 1], AF.Sqrt,
                                            bias=ZEROR, scale=ONER),
                     waits=[(sV, rows_v)])
        vop(lambda v: v.reciprocal(browR[:, 0], brow[:, 2]),
            waits=[(sS, sqrt_s)])
        rows_done = vop(lambda v: v.tensor_mul(
            browR[:, 1], brow[:, 0], browR[:, 0].bitcast(F32)))
        bank_free[id(pROW)] = (sV, rows_done)
        return rows_done

    def bcast_row(row_r_ap, waits):
        mm(pROW[:, :], id(pROW), bonesrow[:, :], row_r_ap, start=True,
           stop=True, extra_waits=waits, incs=[(sT, 1)])
        return sT.n

    def ln_apply(nch, rows_done, src_f32, emit_final, sub_out=None):
        repa = bcast_row(browR[:, 0], [(sV, rows_done)])
        mm(pRC[:, :], id(pRC), bonesrow[:, :], browR[:, 1], start=True,
           stop=True, extra_waits=[(sV, rows_done)], incs=[(sT, 1)])
        repb = sT.n
        for ch in range(nch):
            sl = ch % 2
            r = vop(lambda v, c=ch, q=sl: v.tensor_mul(
                bp6[:, q], src_f32(c), pROW[:, :]),
                waits=[(sT, repb), p6_free[sl]])
            if sub_out is not None:
                r = vop(lambda v, c=ch, q=sl, o=sub_out(ch): v.tensor_sub(
                    o, bp6[:, q], pRC[:, :]))
                p6_free[sl] = (sV, r)
            else:
                r = vop(lambda v, c=ch, q=sl: v.tensor_sub(
                    bp6[:, q], bp6[:, q], pRC[:, :]))
            if emit_final is not None:
                emit_final(ch, sl)
        done = sV.n
        bank_free[id(pROW)] = (sV, done)
        bank_free[id(pRC)] = (sV, done)
        return done

    import os as _osp
    SKIP1 = bool(_osp.environ.get("SKIP_P1"))
    SKIP2 = bool(_osp.environ.get("SKIP_P2"))
    for i in range(NT if not SKIP1 else 1):
        t0 = i * TC
        xw = [(sG, XLOAD[i]), (sV, PREP_V), (sGc, GZERO), (sL, RESIDENT)]

        rows1 = _ln(KD, lambda c: bx[:, c].bitcast(F32), lambda c: bx[:, c], xw)

        xn1_done = ln_apply(KD, rows1, lambda c: bx[:, c].bitcast(F32),
                            None, sub_out=lambda c: bxn1[:, c])
        if dbg_heavy:
            emit("scalar", lambda s, tt=t0: s.dma_start(
                dbg_xn1[:, :, tt:tt + TC], bxn1[:, :, :]),
                waits=[(sV, xn1_done)], incs=[(sSt, 16)])

        conv_free = [(sV, 0), (sV, 0)]
        u_done = [0] * KI
        for cp in range(8):
            slot = wr_ctr[0] % 2
            wr_ctr[0] += 1
            emit("sync", lambda sy, sl=slot, c=cp: sy.dma_start(
                bwr[:, sl, 0:KD], w_inB[c].transpose([1, 0, 2, 3])),
                waits=[wr_free[slot], (sG, CASTS_SMALL)], incs=[(sL, 16)])
            rdy = sL.n
            for c2 in range(2):
                ch = cp * 2 + c2
                pp, pk = (pA, id(pA)) if ch % 2 == 0 else (pB, id(pB))
                for k in range(KD):
                    w0 = ([(sL, rdy), (sV, xn1_done)] if k == 0 else [])
                    mm(pp[:, :], pk, bwr[:, slot, k, c2, :], bxn1[:, k],
                       start=(k == 0), stop=(k == KD - 1),
                       extra_waits=w0,
                       incs=([(sT, 1)] if k == KD - 1 else []))
                mm_T = sT.n
                if c2 == 1:
                    wr_free[slot] = (sT, mm_T)
                csl = 0
                r = vop(lambda v, q=csl, p=pp: v.tensor_copy(
                    bconv[:, q, 4:4 + TC], p[:, :]),
                    waits=[(sT, mm_T), conv_free[csl]])
                bank_free[pk] = (sV, r)
                if dbg_heavy:
                    emit("scalar", lambda s, q=csl, c=ch, tt=t0: s.dma_start(
                        dbg_xh[:, c, tt:tt + TC], bconv[:, q, 4:4 + TC]),
                        waits=[(sV, sV.n)], incs=[(sSt, 16)])
                vop(lambda v, q=csl, c=ch: v.tensor_copy(
                    bconv[:, q, 1:4], bhalo[:, c, :]))
                osl = ch % 2
                vop(lambda v, q=csl, c=ch, o=osl: v.tensor_scalar(
                    bp6[:, o], bconv[:, q, 4:4 + TC], bcw[:, c, 3:4],
                    bcb[:, c:c + 1], ALU.mult, ALU.add),
                    waits=[p6_free[osl]])
                for j in range(3):
                    vop(lambda v, q=csl, c=ch, o=osl, jj=j:
                        v.scalar_tensor_tensor(
                            bp6[:, o], bconv[:, q, 1 + jj:1 + jj + TC],
                            bcw[:, c, jj:jj + 1], bp6[:, o],
                            ALU.mult, ALU.add))
                conv_free[csl] = (sV, sV.n + 1)
                conv_v = vop(lambda v, q=csl, c=ch: v.tensor_copy(
                    bhalo[:, c, :], bconv[:, q, 1 + TC:4 + TC]))
                u_done[ch] = sop(lambda s, c=ch, o=osl: s.activation(
                    bu[:, c].bitcast(F32R), bp6[:, o], AF.Silu,
                    bias=ZERO1, scale=ONE1), waits=[(sV, conv_v)])
                p6_free[osl] = (sS, u_done[ch])

        for k in range(KI):
            inc = [(sT, 1)] if k == KI - 1 else []
            mm(pXDB[:, :], id(pXDB), bxpj[:, k, :], bu[:, k].bitcast(F32R),
               start=(k == 0), stop=(k == KI - 1),
               extra_waits=[(sS, u_done[k])], incs=inc)
        xdb_T = sT.n
        vop(lambda v: v.tensor_copy(bdtr[:, :], pXDB[0:64, :]),
            waits=[(sT, xdb_T)])
        vop(lambda v: v.tensor_copy(bBC[:, :], pXDB[64:80, :]))
        bc_done = vop(lambda v: v.tensor_copy(bCsb[:, :], pXDB[96:112, :]))
        bank_free[id(pXDB)] = (sV, bc_done)

        delta_done = [0] * KI
        du_done = [0] * KI

        def filler(c):
            vop(lambda v, cc=c: v.tensor_scalar(
                by[:, cc], bu[:, cc], bdsk[:, cc:cc + 1], None, ALU.mult),
                waits=[(sS, u_done[c])])
            du_done[c] = vop(lambda v, cc=c: v.tensor_mul(
                bu[:, cc].bitcast(F32R), bdelta[:, cc], bu[:, cc]),
                waits=[(sV, delta_done[c])])

        for ch in range(KI):
            pp, pk = (pA, id(pA)) if ch % 2 == 0 else (pB, id(pB))
            mm(pp[:, :], pk, bdtp[:, ch * 128:(ch + 1) * 128], bdtr[:, :],
               start=True, stop=True, extra_waits=[(sV, bc_done)],
               incs=[(sT, 1)])
            dt_T = sT.n
            d0_v = vop(lambda v, c=ch, p=pp: v.tensor_scalar(
                bdelta[:, c].bitcast(F32R), p[:, :], bdtb2[:, c:c + 1], None,
                ALU.add), waits=[(sT, dt_T)])
            bank_free[pk] = (sV, d0_v)
            a_s = sop(lambda s, c=ch: s.activation(
                bp6[:, 0], bdelta[:, c], AF.Abs, bias=ZERO1, scale=ONE1),
                waits=[(sV, d0_v), p6_free[0]])
            e_s = sop(lambda s: s.activation(bp6[:, 1], bp6[:, 0], AF.Exp,
                                             bias=ZERO1, scale=NEG1F),
                      waits=[p6_free[1]])
            p6_free[0] = (sS, e_s)
            if ch >= 1:
                filler(ch - 1)
            p1_v = vop(lambda v: v.tensor_scalar_add(bp6[:, 1], bp6[:, 1], 1.0),
                       waits=[(sS, e_s)])
            l_s = sop(lambda s: s.activation(bp6[:, 0], bp6[:, 1], AF.Ln,
                                             bias=ZERO1, scale=ONE1),
                      waits=[(sV, p1_v), p6_free[0]])
            p6_free[1] = (sS, l_s)
            delta_done[ch] = vop(lambda v, c=ch: v.scalar_tensor_tensor(
                bdelta[:, c].bitcast(F32R), bdelta[:, c], 0.0, bp6[:, 0],
                ALU.max, ALU.add), waits=[(sS, l_s)])
            p6_free[0] = (sV, delta_done[ch])
            if ch == KI - 1:
                filler(ch)

        all_du = sV.n

        import os as _os2
        if _os2.environ.get("RSER") == "scan":
            ser_now[0] = True
        slot_da = [(sV, 0), (sV, 0)]
        slot_dbu = [(sV, 0), (sV, 0)]
        slot_h = [(sGc, 0), (sGc, 0)]
        crep_free = (sV, 0)
        for n in range(NST):
            mm(pRB0[:, :], id(pRB0), beye[:, n:n + 1].broadcast_to([16, 128]),
               bBC[:, :], start=True, stop=True,
               extra_waits=[(sV, bc_done)], incs=[(sT, 1)])
            brep_T = sT.n
            mm(pRC[:, :], id(pRC), beye[:, n:n + 1].broadcast_to([16, 128]),
               bCsb[:, :], start=True, stop=True,
               extra_waits=[(sV, bc_done)], incs=[(sT, 1)])
            crep_T = sT.n
            crep_v = vop(lambda v: v.tensor_copy(bcrep[:, :], pRC[:, :]),
                         waits=[(sT, crep_T), crep_free])
            bank_free[id(pRC)] = (sV, crep_v)
            last_dbu = 0
            scan_of = {}

            def tail(k):
                q = k % 2
                vop(lambda v, qq=q, c=k, nn=n: v.tensor_copy(
                    bcar[:, c, nn:nn + 1], bhh[:, qq, TC - 1:TC]))
                emit("gpsimd", lambda g, qq=q: g.tensor_tensor(
                    bdbu[:, qq], bhh[:, qq], bcrep[:, :], ALU.mult),
                    waits=[(sV, scan_of[k])], incs=[(sGc, 1)])
                mult_g = sGc.n
                slot_h[q] = (sGc, mult_g)
                ya = vop(lambda v, c=k, qq=q: v.tensor_add(
                    by[:, c], by[:, c], bdbu[:, qq]),
                    waits=[(sGc, mult_g)])
                slot_dbu[q] = (sV, ya)

            for ch in range(KI):
                sl = ch % 2
                da_s = sop(lambda s, c=ch, q=sl, nn=n: s.activation(
                    bda[:, q], bdelta[:, c], AF.Exp, bias=ZERO1,
                    scale=bA[:, c, nn:nn + 1]),
                    waits=[(sV, delta_done[ch]), slot_da[sl]])
                vop(lambda v, c=ch, q=sl, p=pRB0: v.tensor_mul(
                    bdbu[:, q], bu[:, c], p[:, :]),
                    waits=[(sV, du_done[ch]), (sT, brep_T), slot_dbu[sl]])
                last_dbu = sV.n
                scan_v = vop(lambda v, q=sl, c=ch, nn=n: v.tensor_tensor_scan(
                    bhh[:, q], bda[:, q], bdbu[:, q], bcar[:, c, nn:nn + 1],
                    ALU.mult, ALU.add),
                    waits=[(sS, da_s), slot_h[sl]])
                slot_da[sl] = (sV, scan_v)
                scan_of[ch] = scan_v
                if ch >= 1:
                    tail(ch - 1)
            vop(lambda v: v.tensor_copy(bp6[:, 2, 0:256], bp6[:, 2, 256:512]))
            tail(KI - 1)
            bank_free[id(pRB0)] = (sV, last_dbu)
            crep_free = (sGc, sGc.n)
        y_scan_done = sV.n
        if _os2.environ.get("RSER") == "scan":
            ser_now[0] = False

        if dbg_heavy:
            emit("scalar", lambda s, tt=t0: s.dma_start(
                dbg_u[:, :, tt:tt + TC], bu[:, :, :]),
                waits=[(sV, all_du)], incs=[(sSt, 16)])
            emit("scalar", lambda s, tt=t0: s.dma_start(
                dbg_dl[:, :, tt:tt + TC], bdelta[:, :, :]),
                waits=[(sV, all_du)], incs=[(sSt, 16)])
            emit("scalar", lambda s, tt=t0: s.dma_start(
                dbg_y[:, :, tt:tt + TC], by[:, :, :]),
                waits=[(sV, y_scan_done)], incs=[(sSt, 16)])

        for cp in range(8, 16):
            slot = wr_ctr[0] % 2
            wr_ctr[0] += 1
            emit("sync", lambda sy, sl=slot, c=cp: sy.dma_start(
                bwr[:, sl, 0:KD], w_inB[c].transpose([1, 0, 2, 3])),
                waits=[wr_free[slot], (sG, CASTS_SMALL)], incs=[(sL, 16)])
            rdy = sL.n
            for c2 in range(2):
                ch = (cp - 8) * 2 + c2
                pp, pk = (pA, id(pA)) if ch % 2 == 0 else (pB, id(pB))
                for k in range(KD):
                    w0 = ([(sL, rdy)] if k == 0 else [])
                    mm(pp[:, :], pk, bwr[:, slot, k, c2, :], bxn1[:, k],
                       start=(k == 0), stop=(k == KD - 1),
                       extra_waits=w0,
                       incs=([(sT, 1)] if k == KD - 1 else []))
                z_T = sT.n
                if c2 == 1:
                    wr_free[slot] = (sT, z_T)
                zsl = ch % 2
                z_s = sop(lambda s, q=zsl, p=pp: s.activation(
                    bp6[:, q], p[:, :], AF.Silu, bias=ZERO1, scale=ONE1),
                    waits=[(sT, z_T), p6_free[zsl]])
                r = vop(lambda v, c=ch, q=zsl: v.tensor_mul(
                    byb[:, c], by[:, c], bp6[:, q]),
                    waits=[(sS, z_s), (sV, y_scan_done)])
                p6_free[zsl] = (sV, r)
                bank_free[pk] = (sV, r)
        yb_done = sV.n

        for dp in range(4):
            slot = wr_ctr[0] % 2
            wr_ctr[0] += 1
            emit("sync", lambda sy, sl=slot, d=dp: sy.dma_start(
                bwr[:, sl], w_outB[d].transpose([1, 0, 2, 3])),
                waits=[wr_free[slot], (sG, CASTS_SMALL)], incs=[(sL, 16)])
            rdy = sL.n
            for d2 in range(2):
                dch = dp * 2 + d2
                pp, pk = (pA, id(pA)) if dch % 2 == 0 else (pB, id(pB))
                for k in range(KI):
                    w0 = ([(sL, rdy), (sV, yb_done)] if k == 0 else [])
                    mm(pp[:, :], pk, bwr[:, slot, k, d2, :], byb[:, k],
                       start=(k == 0), stop=(k == KI - 1),
                       extra_waits=w0,
                       incs=([(sT, 1)] if k == KI - 1 else []))
                op_T = sT.n
                if d2 == 1:
                    wr_free[slot] = (sT, op_T)
                r = vop(lambda v, c=dch, p=pp: v.tensor_copy(
                    bmo[:, c].bitcast(F32R), p[:, :]), waits=[(sT, op_T)])
                bank_free[pk] = (sV, r)
        mo_all = sV.n

        rows2 = _ln(KD, lambda c: bmo[:, c], lambda c: bmo[:, c].bitcast(F32R),
                    [(sV, mo_all)])

        def fin2(ch, sl):
            r = vop(lambda v, c=ch, q=sl: v.tensor_add(
                bx[:, c], bx[:, c].bitcast(F32), bp6[:, q]))
            p6_free[sl] = (sV, r)
        x2_done = ln_apply(KD, rows2, lambda c: bmo[:, c], fin2)
        emit("scalar", lambda s, tt=t0: s.dma_start(
            x2s[:, :, tt:tt + TC], bx[:, :, :].bitcast(F32)),
            waits=[(sV, x2_done)], incs=[(sSt, 16)])
        x2_store = sSt.n
        if debug:
            emit("scalar", lambda s, tt=t0: s.dma_start(
                dbg_x2[:, :, tt:tt + TC], bx[:, :, :].bitcast(F32)),
                waits=[(sV, x2_done)], incs=[(sSt, 16)])

        rows3 = _ln(KD, lambda c: bx[:, c].bitcast(F32), lambda c: bx[:, c],
                    [(sV, x2_done)])

        def fin3(ch, sl):
            xv = sV.n
            emit("scalar", lambda s, c=ch, q=sl, tt=t0: s.dma_start(
                xn3s[:, c, tt:tt + TC], bp6[:, q]),
                waits=[(sV, xv)], incs=[(sSt, 16)])
            stv = sSt.n
            vop(lambda v, q=sl, c=ch: v.reduce_sum(
                bred[:, c:c + 1], bp6[:, q], axis=AX.X))
            p6_free[sl] = (sSt, stv)
            xn3_stores.append(stv)
        xn3_stores = []
        ln3_done = ln_apply(KD, rows3, lambda c: bx[:, c].bitcast(F32), fin3)
        vop(lambda v: v.tensor_copy(bp6[:, 2, 0:256], bp6[:, 2, 256:512]))
        vop(lambda v: v.tensor_add(bpool[:, :], bpool[:, :], bred[:, :]))
        for sl in (0, 1):
            p6_free[sl] = (sSt, xn3_stores[-1])

        if i + 1 < NT:
            emit("gpsimd", lambda g, tt=(i + 1) * TC: g.dma_start(
                bx[:, :, :], xTr[:, :, tt:tt + TC]),
                waits=[(sV, ln3_done), (sSt, x2_store)], incs=[(sG, 16)])
            XLOAD.append(sG.n)

    s1.close()
    s2 = ExitStack()
    e2 = s2.enter_context
    bw1 = e2(nc.sbuf_tensor("bw1", [128, KD, RH], F32R))
    bw2 = e2(nc.sbuf_tensor("bw2", [128, 2, 8], F32R))
    brr = e2(nc.sbuf_tensor("brr", [1, 64], F32))
    bwrow = e2(nc.sbuf_tensor("bwrow", [1, 8], F32R))
    bmsk = e2(nc.sbuf_tensor("bmsk", [1, 8], mybir.dt.uint8))
    bhrt = e2(nc.sbuf_tensor("bhrt", [128, 2, 128], F32R))
    bhb = e2(nc.sbuf_tensor("bhb", [128, 2], F32))
    bxn3b = e2(nc.sbuf_tensor("bxn3b", [128, KD, TC], BF16))
    bxn3f = e2(nc.sbuf_tensor("bxn3f", [128, KD, TC], F32))
    bH = e2(nc.sbuf_tensor("bH", [128, KF, TC], BF16))
    bzac = e2(nc.sbuf_tensor("bzac", [128, KD, TC], F32))
    bwa = e2(nc.sbuf_tensor("bwa", [128, 4, KD, 2, 128], BF16))
    bwb = e2(nc.sbuf_tensor("bwb", [128, 3, KF, 2, 128], BF16))

    P1BAR = [(sV, sV.n), (sT, sT.n), (sS, sS.n), (sGc, sGc.n),
             (sL, sL.n), (sG, sG.n), (sSt, sSt.n)]

    import os as _os
    if _os.environ.get("RSER"):
        ser_now[0] = True
    emit("gpsimd", lambda g: g.dma_start(
        bw1[:, :, :], w1h.rearrange("(c p) e -> p c e", p=128)),
        waits=P1BAR, incs=[(sG, 16)])
    emit("gpsimd", lambda g: g.dma_start(
        bw2[:, :, :], w2h.rearrange("(c p) e -> p c e", p=128)),
        waits=P1BAR, incs=[(sG, 16)])
    emit("gpsimd", lambda g: g.dma_start(
        bhrt[:, :, :].rearrange("p a b -> p (a b)"), z2x128),
        waits=P1BAR, incs=[(sG, 16)])
    W12 = sG.n
    pool_v = vop(lambda v: v.tensor_scalar_mul(
        bpoolr[:, :], bpool[:, :], 1.0 / L), waits=P1BAR)
    for j in range(2):
        for k in range(KD):
            inc = [(sT, 1)] if (j == 1 and k == KD - 1) else []
            mm(pROW[:, j * 8:(j + 1) * 8], id(pROW),
               bw1[:, k, j * 128:(j + 1) * 128],
               bpoolr[:, k:k + 1].broadcast_to([128, 8]),
               start=(j == 0 and k == 0), stop=(k == KD - 1),
               extra_waits=[(sV, pool_v), (sG, W12)], incs=inc)
    rt_T = sT.n

    def vpad():
        return vop(lambda v: v.tensor_copy(bp6[:, 2, 0:256],
                                           bp6[:, 2, 256:512]))

    for j in range(2):
        vop(lambda v, jj=j: v.tensor_scalar(
            bhb[:, jj:jj + 1], pROW[:, jj * 8:jj * 8 + 1],
            bb1[:, jj:jj + 1], None, ALU.add), waits=[(sT, rt_T)])
    hb_v = vpad()
    h_s = sop(lambda s: s.activation(
        bhrt[:, :, 0], bhb[:, :], AF.Tanh, bias=ZERO1, scale=ONE1),
        waits=[(sV, hb_v), (sG, W12)])
    bank_free[id(pROW)] = (sS, h_s)
    for j in range(2):
        inc = [(sT, 1)] if j == 1 else []
        mm(pXDB[:, 0:8], id(pXDB), bhrt[:, j, :], bw2[:, j, :],
           start=(j == 0), stop=(j == 1), extra_waits=[(sS, h_s)], incs=inc)
    lg_T = sT.n
    vop(lambda v: v.tensor_add(brr[:, 0:8], pXDB[0:1, 0:8], bb2[:, :]),
        waits=[(sT, lg_T)])
    bank_free[id(pXDB)] = (sV, sV.n)
    vpad()
    vop(lambda v: v.reduce_max(brr[:, 8:9], brr[:, 0:8], axis=AX.X))
    vpad()
    vop(lambda v: v.tensor_sub(brr[:, 0:8], brr[:, 0:8],
                               brr[:, 8:9].broadcast_to([1, 8])))
    sm_v = vpad()
    sm_s = sop(lambda s: s.activation(brr[:, 16:24], brr[:, 0:8], AF.Exp,
                                      bias=ZEROR, scale=ONER),
               waits=[(sV, sm_v)])
    vop(lambda v: v.reduce_sum(brr[:, 8:9], brr[:, 16:24], axis=AX.X),
        waits=[(sS, sm_s)])
    vpad()
    vop(lambda v: v.reciprocal(brr[:, 9:10], brr[:, 8:9]))
    vpad()
    vop(lambda v: v.tensor_mul(brr[:, 16:24], brr[:, 16:24],
                               brr[:, 9:10].broadcast_to([1, 8])))
    vpad()
    vop(lambda v: v.tensor_copy(brr[:, 24:32], brr[:, 16:24]))
    vpad()
    for rnd in range(3):
        vop(lambda v: v.reduce_max(brr[:, 8:9], brr[:, 24:32], axis=AX.X))
        vpad()
        if rnd < 2:
            vop(lambda v: v.tensor_tensor(
                bmsk[:, :], brr[:, 24:32],
                brr[:, 8:9].broadcast_to([1, 8]), ALU.is_ge))
            vpad()
            vop(lambda v: v.select(brr[:, 24:32], bmsk[:, :],
                                   bninf[:, :], brr[:, 24:32]))
            vpad()
    vop(lambda v: v.tensor_tensor(brr[:, 32:40], brr[:, 16:24],
                                  brr[:, 8:9].broadcast_to([1, 8]), ALU.is_ge))
    vpad()
    vop(lambda v: v.tensor_mul(brr[:, 24:32], brr[:, 16:24], brr[:, 32:40]))
    vpad()
    vop(lambda v: v.reduce_sum(brr[:, 8:9], brr[:, 24:32], axis=AX.X))
    vpad()
    vop(lambda v: v.reciprocal(brr[:, 9:10], brr[:, 8:9]))
    vpad()
    vop(lambda v: v.tensor_mul(
        bwrow[:, :], brr[:, 24:32],
        brr[:, 9:10].broadcast_to([1, 8])))
    wrow_v = vpad()
    mm(pROW[:, 0:8], id(pROW), bonesrow[:, :], bwrow[:, :],
       start=True, stop=True, extra_waits=[(sV, wrow_v)], incs=[(sT, 1)])
    wz_T = sT.n
    vop(lambda v: v.tensor_copy(bwz[:, :], pROW[:, 0:8]),
        waits=[(sT, wz_T)])
    WZ = vpad()
    bank_free[id(pROW)] = (sV, WZ)
    if debug:
        emit("scalar", lambda s: s.dma_start(dbg_rr, brr[:, :]),
             waits=[(sV, WZ)], incs=[(sSt, 16)])
        emit("scalar", lambda s: s.dma_start(dbg_pool, bpoolr[:, :].bitcast(F32)),
             waits=[(sV, WZ)], incs=[(sSt, 16)])
        emit("scalar", lambda s: s.dma_start(
            dbg_h, bhrt[:, :, 0].bitcast(F32), allow_slow_non_contiguous=True),
             waits=[(sV, WZ)], incs=[(sSt, 16)])
        emit("scalar", lambda s: s.dma_start(dbg_w, bwz[:, :]),
             waits=[(sV, WZ)], incs=[(sSt, 16)])

    if _os.environ.get("RSER") == "router":
        ser_now[0] = False
    wa_free = [(sT, 0), (sT, 0), (sT, 0), (sT, 0)]
    wb_free = [(sT, 0), (sT, 0), (sT, 0)]
    h_free_T = 0
    xn3_used = 0
    x2_used = 0
    wa_ctr = [0]
    wb_ctr = [0]
    for st in range(NT if not SKIP2 else 1):
        t0 = st * TC
        bar = P1BAR if st == 0 else []
        emit("sync", lambda sy, tt=t0: sy.dma_start(
            bxn3f[:, :, :], xn3s[:, :, tt:tt + TC]),
            waits=list(bar) + [(sV, xn3_used)], incs=[(sL, 16)])
        xn3f_ld = sL.n
        emit("gpsimd", lambda g, tt=t0: g.dma_start(
            bx[:, :, :], x2s[:, :, tt:tt + TC]),
            waits=list(bar) + [(sV, x2_used)], incs=[(sG, 16)])
        x2_ld = sG.n
        for ch in range(KD):
            vop(lambda v, c=ch: v.tensor_copy(bxn3b[:, c], bxn3f[:, c]),
                waits=[(sL, xn3f_ld)])
        xn3b_v = sV.n

        for ez in range(NZ):
            h_done = [0] * KF
            wa_rdy = 0
            for fc in range(KF):
                if fc % 2 == 0:
                    slot = wa_ctr[0] % 4
                    wa_ctr[0] += 1
                    emit("sync", lambda sy, z=ez, f=fc, sl=slot: sy.dma_start(
                        bwa[:, sl], waB[z, f // 2].transpose([1, 0, 2, 3])),
                        waits=list(bar) + [wa_free[slot], (sG, CASTS_DONE)],
                        incs=[(sL, 16)])
                    wa_rdy = sL.n
                    cur_wa = slot
                pp, pk = [(pA, id(pA)), (pB, id(pB)),
                          (pROW, id(pROW))][fc % 3]
                for k in range(KD):
                    w0 = ([(sL, wa_rdy), (sV, xn3b_v)] if k == 0 else [])
                    mm(pp[:, :], pk, bwa[:, cur_wa, k, fc % 2, :], bxn3b[:, k],
                       start=(k == 0), stop=(k == KD - 1),
                       extra_waits=w0,
                       incs=([(sT, 1)] if k == KD - 1 else []))
                hT = sT.n
                if fc % 2 == 1:
                    wa_free[cur_wa] = (sT, hT)
                h_done[fc] = sop(lambda s, f=fc, p=pp: s.activation(
                    bH[:, f], p[:, :], AF.Gelu_apprx_tanh,
                    bias=ZERO1, scale=ONE1), waits=[(sT, hT)])
                bank_free[pk] = (sS, h_done[fc])
            for dp in range(4):
                slot = wb_ctr[0] % 3
                wb_ctr[0] += 1
                emit("sync", lambda sy, z=ez, d=dp, sl=slot: sy.dma_start(
                    bwb[:, sl], wbB[z, d].transpose([1, 0, 2, 3])),
                    waits=list(bar) + [wb_free[slot], (sG, CASTS_DONE)],
                    incs=[(sL, 16)])
                wb_rdy = sL.n
                for d2 in range(2):
                    dch = dp * 2 + d2
                    pp, pk = [(pRB0, id(pRB0)), (pRC, id(pRC)),
                              (pXDB, id(pXDB))][dch % 3]
                    for fc in range(KF):
                        w0 = ([(sL, wb_rdy), (sS, h_done[KF - 1])]
                              if fc == 0 else [])
                        mm(pp[:, :], pk, bwb[:, slot, fc, d2, :], bH[:, fc],
                           start=(fc == 0), stop=(fc == KF - 1),
                           extra_waits=w0,
                           incs=([(sT, 1)] if fc == KF - 1 else []))
                    zT = sT.n
                    if d2 == 1:
                        wb_free[slot] = (sT, zT)
                    if ez == 0:
                        r = vop(lambda v, c=dch, p=pp: v.scalar_tensor_tensor(
                            bzac[:, c].bitcast(F32R), p[:, :], bwz[:, 0:1],
                            bxn3f[:, c], ALU.mult, ALU.add),
                            waits=[(sT, zT), (sV, WZ), (sL, xn3f_ld)])
                    else:
                        r = vop(lambda v, c=dch, p=pp, z=ez:
                                v.scalar_tensor_tensor(
                                    bzac[:, c].bitcast(F32R), p[:, :],
                                    bwz[:, z:z + 1], bzac[:, c],
                                    ALU.mult, ALU.add), waits=[(sT, zT)])
                    bank_free[pk] = (sV, r)
            h_free_T = sT.n
        xn3_used = sV.n

        f_done = sV.n

        rows4 = _ln(KD, lambda c: bzac[:, c],
                    lambda c: bzac[:, c].bitcast(F32R), [(sV, f_done)])

        def fin4(ch, sl):
            r = vop(lambda v, c=ch, q=sl: v.tensor_add(
                bx[:, c], bx[:, c].bitcast(F32), bp6[:, q]),
                waits=[(sG, x2_ld)])
            p6_free[sl] = (sV, r)
        out_v = ln_apply(KD, rows4, lambda c: bzac[:, c], fin4)
        x2_used = out_v
        emit("scalar", lambda s, tt=t0: s.dma_start(
            out[:, :, tt:tt + TC], bx[:, :, :].bitcast(F32)),
            waits=[(sV, out_v)], incs=[(sSt, 16)])

    FIN = sSt.n

    def fin_wait(h):
        h.wait_ge(sSt.h, FIN)
    OPS["scalar"].append(fin_wait)

    s2.close()
    build.LBL = LBL
    with nc.allow_low_precision(reason="bf16/f32r matmul operands; fp32 accumulation everywhere"), nc.Block() as blk:
        @blk.sync
        def _(h):
            for f in OPS["sync"]:
                f(h)

        @blk.gpsimd
        def _(h):
            for f in OPS["gpsimd"]:
                f(h)

        @blk.tensor
        def _(h):
            for f in OPS["tensor"]:
                f(h)

        @blk.vector
        def _(h):
            for f in OPS["vector"]:
                f(h)

        @blk.scalar
        def _(h):
            for f in OPS["scalar"]:
                f(h)

    es.close()
    return nc


def _prep_weights(inp):
    f32 = np.float32
    w = {}
    ipw = np.asarray(inp["L0_attn_in_proj_w"], f32)
    w_inT = np.ascontiguousarray(ipw.T)
    w["w_inT"] = np.ascontiguousarray(
        w_inT.reshape(KD, 128, 16, 2, 128).transpose(2, 0, 1, 3, 4))
    opw = np.asarray(inp["L0_attn_out_proj_w"], f32)
    w_outT = np.ascontiguousarray(opw.T)
    w["w_outT"] = np.ascontiguousarray(
        w_outT.reshape(KI, 128, 4, 2, 128).transpose(2, 0, 1, 3, 4))
    xp = np.asarray(inp["L0_attn_x_proj_w"], f32).T
    xpp = np.zeros((DI, 128), f32)
    xpp[:, 0:64] = xp[:, 0:64]
    xpp[:, 64:80] = xp[:, 64:80]
    xpp[:, 96:112] = xp[:, 80:96]
    w["w_xpT"] = np.ascontiguousarray(
        xpp.reshape(KI, 128, 128).transpose(1, 0, 2))
    w["w_dtT"] = np.ascontiguousarray(np.asarray(inp["L0_attn_dt_proj_w"], f32).T)
    w["convw"] = np.ascontiguousarray(
        np.asarray(inp["L0_attn_conv_w"], f32)[:, 0, :]
        .reshape(KI, 128, 4).transpose(1, 0, 2))
    w["convb"] = np.ascontiguousarray(
        np.asarray(inp["L0_attn_conv_b"], f32).reshape(KI, 128).T)
    w["dtb"] = np.ascontiguousarray(
        np.asarray(inp["L0_attn_dt_bias"], f32).reshape(KI, 128).T)
    w["alog"] = np.ascontiguousarray(
        np.asarray(inp["L0_attn_A_log"], f32).reshape(KI, 128, NST)
        .transpose(1, 0, 2))
    w["dsk"] = np.ascontiguousarray(
        np.asarray(inp["L0_attn_D_skip"], f32).reshape(KI, 128).T)
    w["w1h"] = np.ascontiguousarray(np.asarray(inp["L0_ffn_W1"], f32))
    w["b1c"] = np.ascontiguousarray(
        np.asarray(inp["L0_ffn_b1"], f32).reshape(2, 128).T)
    w["w2h"] = np.ascontiguousarray(np.asarray(inp["L0_ffn_W2"], f32))
    w["b2r"] = np.ascontiguousarray(np.asarray(inp["L0_ffn_b2"], f32).reshape(1, 8))
    Wa = np.asarray(inp["L0_ffn_Wa"], f32)
    w["wah"] = np.ascontiguousarray(
        Wa.reshape(NZ, KD, 128, 16, 2, 128).transpose(0, 3, 1, 2, 4, 5))
    Wb = np.asarray(inp["L0_ffn_Wb"], f32)
    w["wbh"] = np.ascontiguousarray(
        Wb.reshape(NZ, KF, 128, 4, 2, 128).transpose(0, 3, 1, 2, 4, 5))
    cstc = np.zeros((128, 8), f32)
    cstc[:, 0] = 1.0
    cstc[:, 2] = -1.0
    cstc[:, 5] = 1.0
    w["cst"] = cstc
    cms = np.zeros((128, 256), f32)
    cms[:, 0] = 1.0
    cms[:, 128 + 32] = 1.0
    w["cstMS"] = cms
    w["z2x128"] = np.zeros((128, 256), f32)
    w["eyeh"] = np.eye(16, dtype=f32)
    w["onesrow"] = np.ones((1, 128), f32)
    w["ninf"] = np.full((1, 8), -1e30, f32)
    return w


def kernel(**inputs):
    x = np.asarray(inputs["L0_attn_hidden_states"], np.float32)
    B, L, _ = x.shape
    wts = _prep_weights(inputs)
    nc = build(L)
    in_maps = []
    for b in range(B):
        m = dict(wts)
        m["xT"] = np.ascontiguousarray(x[b].T)
        in_maps.append(m)
    res = run_bass_kernel_spmd(nc, in_maps, core_ids=list(range(B)))
    outs = []
    for r in res.results:
        o = np.asarray(r["out"]).reshape(128, KD, L)
        od = np.ascontiguousarray(o.transpose(1, 0, 2).reshape(D, L).T)
        outs.append(od)
    return np.stack(outs).astype(np.float32)
